# version 15
# baseline (speedup 1.0000x reference)
import numpy as np
import ml_dtypes
from contextlib import ExitStack

import concourse.bass as bass
import concourse.mybir as mybir
from concourse.bass_utils import run_bass_kernel_spmd

F32 = mybir.dt.float32
BF16 = mybir.dt.bfloat16
AF = mybir.ActivationFunctionType
ALU = mybir.AluOpType
AX = mybir.AxisListType

D = 1024
NF = 8
NH = 16
HD = 64
NP = 8
CH = 128
DFF = 4096
NFF = 32
NE = 6
NL = 4
MOE_I = 1024
LN_EPS = 1e-5
ATT_EPS = 1e-6
GATE_EPS = 1e-8

TSA = 512
TSB = 256
TSM = 512
SKIP_LDW = True
ATTN_SECTIONS = {"preln", "qk", "v", "ktr", "attn_sc", "attn_ae", "attn_up", "attn_tr", "wo", "ln1", "store"}

ENGINES = ("sync", "gpsimd", "tensor", "vector", "scalar")


class Buf:
    __slots__ = ("w", "r")

    def __init__(self):
        self.w = None
        self.r = []

    def rd(self):
        return [self.w] if self.w else []

    def wr(self):
        d = list(self.r)
        if self.w:
            d.append(self.w)
        return d


class Prog:
    def __init__(self, nc, es):
        self.nc = nc
        self.es = es
        self.q = {e: [] for e in ENGINES}
        self.sems = {}
        self.cnt = {}
        self.est = {}
        self.bufs = {}
        self.pending = {}
        for s in ("sT", "sV", "sS", "sG"):
            self.sem(s)

    def sem(self, s):
        if s not in self.sems:
            self.sems[s] = self.es.enter_context(self.nc.semaphore(s))
            self.cnt[s] = 0
        return s

    def level(self, keys, s):
        tok = (s, self.cnt[s])
        for k in keys:
            self.buf(k).w = tok

    def buf(self, key):
        b = self.bufs.get(key)
        if b is None:
            b = self.bufs[key] = Buf()
        return b

    def _emit(self, eng, fn, dep, inc, n):
        if inc is not None:
            self.sem(inc)
        waits = []
        for t in dep:
            if t is None:
                continue
            s, v = t
            if self.est.get((eng, s), 0) < v:
                waits.append((s, v))
                self.est[(eng, s)] = v
        tok = None
        if inc is not None:
            self.sem(inc)
            self.cnt[inc] += n
            tok = (inc, self.cnt[inc])
        sems = self.sems

        def run(e, fn=fn, waits=tuple(waits), tok=tok, n=n):
            for s, v in waits:
                e.wait_ge(sems[s], v)
            ins = fn(e)
            if tok is not None:
                ins.then_inc(sems[tok[0]], n)

        self.q[eng].append(run)
        return tok

    def op(self, eng, fn, reads=(), writes=(), deps=(), inc=None, n=1):
        dep = list(deps) + list(self.pending.pop(eng, ()))
        for k in reads:
            dep += self.buf(k).rd()
        for k in writes:
            dep += self.buf(k).wr()
        tok = self._emit(eng, fn, dep, inc, n)
        for k in reads:
            if tok is not None:
                self.buf(k).r.append(tok)
        for k in writes:
            b = self.buf(k)
            b.w = tok
            b.r = []
        return tok

    def group(self, eng, fns, reads=(), writes=(), deps=(), inc="sT", n=1):
        assert fns
        if len(fns) == 1:
            return self.op(eng, fns[0], reads=reads, writes=writes, deps=deps,
                           inc=inc, n=n)
        dep = list(deps) + list(self.pending.pop(eng, ()))
        for k in reads:
            dep += self.buf(k).rd()
        for k in writes:
            dep += self.buf(k).wr()
        self._emit(eng, fns[0], dep, None, 1)
        for fn in fns[1:-1]:
            self._emit(eng, fn, (), None, 1)
        tok = self._emit(eng, fns[-1], (), inc, n)
        for k in reads:
            if tok is not None:
                self.buf(k).r.append(tok)
        for k in writes:
            b = self.buf(k)
            b.w = tok
            b.r = []
        return tok

    def barrier_all(self):
        toks = [(s, c) for s, c in self.cnt.items() if c > 0]
        for e in ENGINES:
            self.pending[e] = list(toks)


class Arena:
    def __init__(self, nc, base, top):
        self.nc = nc
        self.base = (base + 31) // 32 * 32
        self.top = top
        self.off = self.base
        self.uid = 0

    def alloc(self, name, shape, dtype):
        self.off = (self.off + 31) // 32 * 32
        t = self.nc.alloc_sbuf_tensor_at(f"{name}_{self.uid}", shape, dtype,
                                         offset=self.off)
        self.uid += 1
        self.off += int(np.prod(shape[1:])) * mybir.dt.size(dtype)
        assert self.off <= self.top, (
            f"SBUF overflow at {name}: used {self.off - self.base} of "
            f"{self.top - self.base}")
        return t

    def alloc_at(self, name, shape, dtype, off):
        t = self.nc.alloc_sbuf_tensor_at(f"{name}_{self.uid}", shape, dtype,
                                         offset=(off + 31) // 32 * 32)
        self.uid += 1
        return t

    def reset(self, off):
        self.off = off


def build_program(n_tok, n_seq, dbg=False, upto=99):
    seq = n_tok // n_seq
    assert seq % TSA == 0 and n_tok % TSB == 0 and n_tok % TSM == 0
    n_sup = n_tok // TSA
    sup_seq = seq // TSA
    nb = n_tok // TSB
    nm = n_tok // TSM

    nc = bass.Bass("TRN2", target_bir_lowering=False)

    x_in = nc.dram_tensor("x", [NF, 128, n_tok], F32, kind="ExternalInput").ap()
    wq, wk, wv, wo, w1, w2 = [], [], [], [], [], []
    for l in range(NL):
        wq.append(nc.dram_tensor(f"wq{l}", [NF, 128, D], F32, kind="ExternalInput").ap())
        wk.append(nc.dram_tensor(f"wk{l}", [NF, 128, D], F32, kind="ExternalInput").ap())
        wv.append(nc.dram_tensor(f"wv{l}", [NF, 128, D], F32, kind="ExternalInput").ap())
        wo.append(nc.dram_tensor(f"wo{l}", [NF, 128, D], F32, kind="ExternalInput").ap())
        w1.append(nc.dram_tensor(f"w1_{l}", [NF, 128, DFF], F32, kind="ExternalInput").ap())
        w2.append(nc.dram_tensor(f"w2_{l}", [NFF, 128, D], F32, kind="ExternalInput").ap())
    vecs_in = nc.dram_tensor("vecs", [128, 416], F32, kind="ExternalInput").ap()
    rows_in = nc.dram_tensor("rows", [1, 128 + NL * D], F32, kind="ExternalInput").ap()
    mask_in = nc.dram_tensor("maskT", [128, 128], F32, kind="ExternalInput").ap()
    eye_in = nc.dram_tensor("eye", [128, 128], F32, kind="ExternalInput").ap()
    onec_in = nc.dram_tensor("onec", [128, 1], F32, kind="ExternalInput").ap()
    ew1t = nc.dram_tensor("ew1t", [NE, NF, 128, MOE_I], F32, kind="ExternalInput").ap()
    ew3t = nc.dram_tensor("ew3t", [NE, NF, 128, MOE_I], F32, kind="ExternalInput").ap()
    ew2t = nc.dram_tensor("ew2t", [NE, NF, 128, D], F32, kind="ExternalInput").ap()
    fc1t = nc.dram_tensor("fc1t", [NF, 128, MOE_I], F32, kind="ExternalInput").ap()
    fc2t = nc.dram_tensor("fc2t", [NF, 128, MOE_I], F32, kind="ExternalInput").ap()
    fc3t = nc.dram_tensor("fc3t", [NF, 128, D], F32, kind="ExternalInput").ap()
    gwt = nc.dram_tensor("gwt", [NF, 128, NE], F32, kind="ExternalInput").ap()

    out = nc.dram_tensor("out", [NF, 128, n_tok], F32, kind="ExternalOutput").ap()
    dkind = "ExternalOutput" if dbg else "Internal"
    d_ln = nc.dram_tensor("d_ln", [NF, 128, n_tok], F32, kind=dkind).ap()
    d_a = nc.dram_tensor("d_a", [NF, 128, n_tok], F32, kind=dkind).ap()
    d_b = nc.dram_tensor("d_b", [NF, 128, n_tok], F32, kind=dkind).ap()
    d_xm = nc.dram_tensor("d_xm", [NF, 128, n_tok], BF16, kind=dkind).ap()
    d_y = nc.dram_tensor("d_y", [NF, 128, n_tok], F32, kind=dkind).ap()
    d_h = nc.dram_tensor("d_h", [NFF, 128, n_tok], BF16, kind="Internal").ap()
    d_ge = nc.dram_tensor("d_ge", [NE, n_tok], F32, kind=dkind).ap()

    def hbm(t):
        return t.rearrange("f p t -> p f t")

    es = ExitStack()
    with es:
        P = Prog(nc, es)

        A = Arena(nc, nc.sbuf_base, nc.sbuf_top)
        sb_vecs = A.alloc("vecs", [128, 416], F32)
        sb_rows = A.alloc("rows", [1, 128], F32)
        sb_bvrow = A.alloc("bvrow", [1, D], F32)
        sb_mask = A.alloc("mask", [128, 128], F32)
        sb_eye32 = A.alloc("eye32", [128, 128], F32)
        sb_eyebf = A.alloc("eyebf", [128, 128], BF16)
        sb_onesbf = A.alloc("onesbf", [128, 1], BF16)
        sb_negqk = A.alloc("negqk", [128, NL * 16], F32)
        sb_bvb = A.alloc("bvb", [128, D], F32)
        sb_zrow = A.alloc("zrow", [1, 288], BF16)
        sb_mrow = A.alloc("mrow", [1, TSA], F32)
        sb_rrow = A.alloc("rrow", [1, TSA], F32)
        sb_q2row = A.alloc("q2row", [1, TSA], F32)
        sb_trow = A.alloc("trow", [1, TSA], F32)
        WREG = 8192
        wpool = A.alloc("wpool", [128, 8 * WREG], BF16)
        wpool_off = A.off - 8 * WREG * 2
        scratchA = wpool_off + 4 * WREG * 2
        scratchB = A.off
        scratchC = wpool_off + 1 * WREG * 2
        scratchE = wpool_off + 3 * WREG * 2

        BANK = {
            "pA": nc.alloc_psum_tensor("pA", [128, 512], F32),
            "pB": nc.alloc_psum_tensor("pB", [128, 512], F32),
            "pC": nc.alloc_psum_tensor("pC", [128, 512], F32),
            "pD": nc.alloc_psum_tensor("pD", [128, 512], F32),
            "pE": nc.alloc_psum_tensor("pE", [128, 512], F32),
            "pF": nc.alloc_psum_tensor("pF", [128, 512], F32),
            "pG": nc.alloc_psum_tensor("pG", [128, 1024], BF16),
            "pH": nc.alloc_psum_tensor("pH", [128, 1024], BF16),
        }

        def wsl(reg, c0, c1):
            return wpool[:, reg * WREG + c0: reg * WREG + c1]

        def vl(l):
            return 96 * l
        VC = dict(bq=0, bk=8, bo=16, b1=24, b2=56, g1=64, be1=72, g2=80, be2=88)
        GF_C, BEF_C, GB_C = 384, 392, 400

        def vcol(c):
            return sb_vecs[:, c:c + 1]

        def vcols(l, nm_):
            base = vl(l) + VC[nm_]
            return [vcol(base + f) for f in range(NF)]

        def emit_ln(T, nf, src_ap, src_key, cp_ap, dst_ap, dst_key,
                    gain, stat_bank, bc_bank, tmp=None, bf16_dst=None,
                    bf16_key=None, rows=None, defer=None):
            sb = BANK[stat_bank]
            sb2 = BANK[bc_bank]
            if rows is None:
                r_m, r_q, r_t, r_r, rtag = sb_mrow, sb_q2row, sb_trow, sb_rrow, ""
            else:
                r_m, r_q, r_t, r_r, rtag = rows
            if T <= 256:
                s2ap = sb[0:1, 256:256 + T]
                s2key = stat_bank
                mb = sb2[:, 0:T]
                rb = sb2[:, 256:256 + T]
                rbkey = bc_bank
            else:
                s2ap = sb2[0:1, 0:T]
                s2key = bc_bank
                mb = sb2[:, 0:T]
                rb = sb[:, 0:T]
                rbkey = stat_bank

            def st1():
                P.op("scalar", lambda e: e.activation(cp_ap, src_ap, AF.Copy),
                     reads=[src_key], writes=["lncp" + rtag], inc="sS")
                mms = [lambda e, f=f: e.matmul(
                    sb[0:1, 0:T], sb_onesbf[:, 0:1], cp_ap[:, f * T:(f + 1) * T],
                    start=(f == 0), stop=(f == nf - 1)) for f in range(nf)]
                P.group("tensor", mms, reads=["lncp" + rtag, "onesbf"], writes=[stat_bank])
                P.op("scalar", lambda e: e.activation(cp_ap, src_ap, AF.Square),
                     reads=[src_key], writes=["lncp" + rtag], inc="sS")
                mms = [lambda e, f=f: e.matmul(
                    s2ap, sb_onesbf[:, 0:1], cp_ap[:, f * T:(f + 1) * T],
                    start=(f == 0), stop=(f == nf - 1)) for f in range(nf)]
                st1.tok = P.group("tensor", mms, reads=["lncp" + rtag, "onesbf"],
                                  writes=[s2key])

            def st2():
                P.op("vector", lambda e: e.tensor_scalar_mul(
                    r_m[:, 0:T], sb[0:1, 0:T], 1.0 / (nf * 128)),
                    reads=[stat_bank], writes=["rbm" + rtag], deps=[st1.tok], inc="sV")
                P.op("vector", lambda e: e.tensor_scalar(
                    r_q[:, 0:T], s2ap, 1.0 / (nf * 128), LN_EPS,
                    ALU.mult, ALU.add),
                    reads=[s2key], writes=["rbq" + rtag], inc="sV")
                P.op("vector", lambda e: e.tensor_tensor(
                    r_t[:, 0:T], r_m[:, 0:T], r_m[:, 0:T], ALU.mult),
                    reads=["rbm" + rtag], writes=["rbt" + rtag], inc="sV")
                P.op("vector", lambda e: e.scalar_tensor_tensor(
                    r_t[:, 0:T], r_t[:, 0:T], -1.0, r_q[:, 0:T],
                    ALU.mult, ALU.add),
                    reads=["rbq" + rtag, "rbt" + rtag], writes=["rbt" + rtag], inc="sV")
                P.op("scalar", lambda e: e.activation(
                    r_t[:, 0:T], r_t[:, 0:T], AF.Ln),
                    reads=["rbt" + rtag], writes=["rbt" + rtag], inc="sS")
                P.op("scalar", lambda e: e.activation(
                    r_r[:, 0:T], r_t[:, 0:T], AF.Exp, scale=-0.5),
                    reads=["rbt" + rtag], writes=["rbr" + rtag], inc="sS")
                P.op("vector", lambda e: e.tensor_tensor(
                    r_m[:, 0:T], r_m[:, 0:T], r_r[:, 0:T], ALU.mult),
                    reads=["rbm" + rtag, "rbr" + rtag], writes=["rbm" + rtag], inc="sV")

            def st3():
                P.op("tensor", lambda e, mb=mb: e.matmul(
                    mb, sb_rows[0:1, 0:128], r_m[:, 0:T], start=True, stop=True),
                    reads=["rbm" + rtag, "rows"], writes=[bc_bank], inc="sT")
                P.op("tensor", lambda e, rb=rb: e.matmul(
                    rb, sb_rows[0:1, 0:128], r_r[:, 0:T], start=True, stop=True),
                    reads=["rbr" + rtag, "rows"], writes=[rbkey], inc="sT")

            def st4():
                if tmp is None:
                    t_ap, t_key = dst_ap, dst_key
                else:
                    t_ap, t_key = tmp
                mb3 = mb.unsqueeze(1).broadcast_to((128, nf, T))
                rb3 = rb.unsqueeze(1).broadcast_to((128, nf, T))
                s3 = src_ap.rearrange("p (f t) -> p f t", f=nf)
                t3 = t_ap.rearrange("p (f t) -> p f t", f=nf)
                P.op("vector", lambda e: e.tensor_tensor(t3, s3, rb3, ALU.mult),
                     reads=[src_key, rbkey], writes=[t_key], inc="sV")
                d3 = dst_ap.rearrange("p (f t) -> p f t", f=nf)
                P.op("vector", lambda e: e.tensor_tensor(d3, t3, mb3, ALU.subtract),
                     reads=[t_key, bc_bank], writes=[dst_key], inc="sV")
                if gain is not None:
                    gl, bl = gain
                    for f in range(nf):
                        P.op("vector", lambda e, f=f: e.tensor_scalar(
                            dst_ap[:, f * T:(f + 1) * T], dst_ap[:, f * T:(f + 1) * T],
                            gl[f], bl[f], ALU.mult, ALU.add),
                            reads=[dst_key, "vecs"], writes=[dst_key], inc="sV")
                if bf16_dst is not None:
                    P.op("scalar", lambda e: e.activation(bf16_dst, dst_ap, AF.Copy),
                         reads=[dst_key], writes=[bf16_key], inc="sS")

            if defer is None:
                st1(); st2(); st3(); st4()
            else:
                defer.extend([st1, st2, st3, st4])
            return None

        P.op("sync", lambda e: e.dma_start(out=sb_vecs[:], in_=vecs_in),
             writes=["vecs"], inc="sP", n=16)
        P.op("sync", lambda e: e.dma_start(out=sb_rows[:], in_=rows_in[0:1, 0:128]),
             writes=["rows"], inc="sP", n=16)
        P.op("sync", lambda e: e.dma_start(out=sb_mask[:], in_=mask_in),
             writes=["mask"], inc="sP", n=16)
        P.op("sync", lambda e: e.dma_start(out=sb_eye32[:], in_=eye_in),
             writes=["eye32"], inc="sP", n=16)
        P.op("gpsimd", lambda e: e.dma_start(out=sb_eyebf[:], in_=eye_in),
             writes=["eyebf"], inc="sPW", n=16)
        P.op("gpsimd", lambda e: e.dma_start(out=sb_onesbf[:], in_=onec_in),
             writes=["onesbf"], inc="sPW", n=16)
        P.level(["vecs", "rows", "mask", "eye32"], "sP")
        P.level(["eyebf", "onesbf"], "sPW")
        P.op("vector", lambda e: e.memset(sb_zrow[:], 0.0),
             writes=["zrow"], inc="sV")
        for l in range(NL):
            P.op("vector", lambda e, l=l: e.tensor_scalar_mul(
                sb_negqk[:, l * 16:(l + 1) * 16],
                sb_vecs[:, vl(l):vl(l) + 16], -1.0),
                reads=["vecs"], writes=["negqk"], inc="sV")

        def phase_attn(l, d_src, src_tag, pre_ln):
            P.barrier_all()
            A.reset(scratchA)
            sb_x = [A.alloc("ax0", [128, NF * TSA], F32),
                    A.alloc("ax1", [128, NF * TSA], F32)]
            sb_xbf = A.alloc("axbf", [128, NF * TSA], BF16)
            qz_off = A.off
            sb_qza = A.alloc("aqza", [128, NF * TSA], BF16)
            sb_qzb = A.alloc("aqzb", [128, NF * TSA], BF16)
            sb_kf = A.alloc("akf", [128, NF * TSA], BF16)
            sb_kta = A.alloc("akta", [128, 4 * D], BF16)
            sb_ktb = A.alloc("aktb", [128, 4 * D], BF16)
            sb_v = A.alloc("av", [128, 4 * NH * 65], BF16)
            sb_sc = A.alloc("asc", [128, NH * CH], BF16)
            sb_att = A.alloc("aatt", [128, NP * CH], BF16)
            sb_attF = A.alloc("aattF", [128, NF * TSA], BF16)
            sb_S = A.alloc("aS", [128, 2 * 260], BF16)
            sb_xa = A.alloc_at("axa", [128, NF * TSA], F32, qz_off)
            sb_cp = A.alloc("acp", [128, NF * TSA], BF16)
            sb_u = A.alloc("au", [128, TSA], F32)
            sb_e = A.alloc("ae", [128, TSA], F32)
            sb_r = A.alloc("ar", [128, TSA], F32)
            sb_zc = A.alloc("azc", [128, 3 * TSA], F32)
            sb_den = A.alloc("aden", [128, 32], F32)
            v4 = sb_v[:].rearrange("p (c h e) -> p c h e", c=4, h=NH)

            for reg, w_ in ((0, wq[l]), (1, wk[l]), (2, wv[l]), (3, wo[l])):
                P.op("gpsimd", lambda e, reg=reg, w_=w_: e.dma_start(
                    out=wsl(reg, 0, NF * D).rearrange("p (f d) -> p f d", f=NF),
                    in_=hbm(w_)), writes=[("w", reg)], inc="sW", n=16)
            P.level([("w", r) for r in range(4)], "sW")

            P.op("sync", lambda e, l=l: e.dma_start(
                out=sb_bvrow[:], in_=rows_in[0:1, 128 + l * D:128 + (l + 1) * D]),
                writes=["bvrow"], inc="srb", n=16)
            bvrow = sb_bvrow[0:1, :]
            for h in range(2):
                P.op("tensor", lambda e, h=h: e.matmul(
                    BANK["pC"][:, 0:512], sb_rows[0:1, 0:128],
                    bvrow[:, h * 512:(h + 1) * 512], start=True, stop=True),
                    reads=["rows", "bvrow"], writes=["pC"], inc="sT")
                P.op("vector", lambda e, h=h: e.tensor_copy(
                    sb_bvb[:, h * 512:(h + 1) * 512], BANK["pC"][:, 0:512]),
                    reads=["pC"], writes=["bvb"], inc="sV")
            P.op("vector", lambda e: e.memset(v4[:, :, :, 64:65], 1.0),
                 writes=["vones"], inc="sV")
            kta4 = sb_kta[:].rearrange("p (c q h d) -> p c q h d", c=4, q=NP, h=2)
            ktb4 = sb_ktb[:].rearrange("p (c q h d) -> p c q h d", c=4, q=NP, h=2)
            P.op("vector", lambda e: e.memset(kta4[:, :, :, 1, :], 0.0),
                 writes=[("aktA", og) for og in range(4)], inc="sV")
            P.op("vector", lambda e: e.memset(ktb4[:, :, :, 0, :], 0.0),
                 writes=[("aktB", og) for og in range(4)], inc="sV")

            g1c, be1c = vcols(l, "g1"), vcols(l, "be1")
            boc = vcols(l, "bo")
            bqc, bkc = vcols(l, "bq"), vcols(l, "bk")
            nbq = [sb_negqk[:, l * 16 + f:l * 16 + f + 1] for f in range(NF)]
            nbk = [sb_negqk[:, l * 16 + 8 + f:l * 16 + 8 + f + 1] for f in range(NF)]

            pending = []
            for u in range(n_sup):
                us = u % sup_seq
                t0 = u * TSA
                xk = ("ax", u % 2)
                P.op("sync", lambda e, t0=t0, u=u: e.dma_start(
                    out=sb_x[u % 2][:].rearrange("p (f t) -> p f t", f=NF),
                    in_=hbm(d_src)[:, :, t0:t0 + TSA]),
                    reads=[(src_tag, u)], writes=[xk], inc=f"sxa{u % 2}", n=16)
                if pre_ln and "preln" in ATTN_SECTIONS:
                    emit_ln(TSA, NF, sb_x[u % 2][:], xk, sb_cp[:],
                            sb_x[u % 2][:], xk, None, "pC", "pD")
                src_f32 = sb_x[u % 2]
                P.op("scalar", lambda e, s=src_f32: e.activation(
                    sb_xbf[:], s[:], AF.Copy),
                    reads=[xk], writes=["axbf"], inc="sS")
                P.op("vector", lambda e: e.memset(sb_qza[64:128, :], 0.0),
                     writes=[("aqA", ot) for ot in range(NF)],
                     deps=P.buf("axa").wr(), inc="sV")
                P.op("vector", lambda e: e.memset(sb_qzb[0:64, :], 0.0),
                     writes=[("aqB", ot) for ot in range(NF)], inc="sV")

                if "qk" in ATTN_SECTIONS:
                    qk_list = ((0, None, None, nbq, bqc), (1, sb_kf, "akf", nbk, bkc))
                else:
                    qk_list = ()
                for wreg, dst, dstk, nb_, bc_ in qk_list:
                    for ot in range(NF):
                        bkk = ("pA", "pB", "pC", "pD")[ot % 4]
                        par = ot % 2
                        mms = [lambda e, kt=kt, ot=ot, bkk=bkk, wreg=wreg: e.matmul(
                            BANK[bkk][:, 0:TSA],
                            wsl(wreg, kt * D + ot * 128, kt * D + (ot + 1) * 128),
                            sb_xbf[:, kt * TSA:(kt + 1) * TSA],
                            start=(kt == 0), stop=(kt == NF - 1))
                            for kt in range(NF)]
                        P.group("tensor", mms, reads=["axbf", ("w", wreg)],
                                writes=[bkk])
                        uap = sb_u[:, 0:TSA]
                        eap = sb_e[:, 0:TSA]
                        rap = sb_r[:, 0:TSA]
                        zslot = (wreg * NF + ot) % 3
                        zc = sb_zc[:, zslot * TSA:(zslot + 1) * TSA]
                        P.op("vector", lambda e, bkk=bkk, zc=zc: e.tensor_copy(
                            zc, BANK[bkk][:, 0:TSA]),
                            reads=[bkk], writes=[("azc", zslot)], inc="sV")
                        if ot % 2 == 0:
                            P.op("scalar", lambda e, zc=zc, ot=ot, nb_=nb_, uap=uap: e.activation(
                                uap, zc, AF.Relu, bias=nb_[ot], scale=-1.0),
                                reads=[("azc", zslot)], writes=["au"], inc="sS")
                            P.op("scalar", lambda e, uap=uap, eap=eap: e.activation(
                                eap, uap, AF.Exp, scale=-1.0),
                                reads=["au"], writes=["ae"], inc="sS")
                            P.op("scalar", lambda e, zc=zc, ot=ot, bc_=bc_, rap=rap: e.activation(
                                rap, zc, AF.Relu, bias=bc_[ot], scale=1.0),
                                reads=[("azc", zslot)], writes=["ar"], inc="sS")
                        else:
                            P.op("vector", lambda e, zc=zc, ot=ot, bc_=bc_, uap=uap: e.tensor_scalar(
                                uap, zc, bc_[ot], 0.0, ALU.add, ALU.min),
                                reads=[("azc", zslot), "vecs"], writes=["au"], inc="sV")
                            P.op("scalar", lambda e, uap=uap, eap=eap: e.activation(
                                eap, uap, AF.Exp),
                                reads=["au"], writes=["ae"], inc="sS")
                            P.op("vector", lambda e, zc=zc, ot=ot, bc_=bc_, uap=uap, rap=rap: e.scalar_tensor_tensor(
                                rap, zc, bc_[ot], uap, ALU.add, ALU.subtract),
                                reads=[("azc", zslot), "au", "vecs"], writes=["ar"], inc="sV")
                        if wreg == 0:
                            P.op("gpsimd", lambda e, ot=ot, eap=eap, rap=rap: e.tensor_add(
                                sb_qza[0:64, ot * TSA:(ot + 1) * TSA],
                                eap[0:64, :], rap[0:64, :]),
                                reads=["ae", "ar"],
                                writes=[("aqA", ot)], inc="sG")
                            P.op("gpsimd", lambda e, ot=ot, eap=eap, rap=rap: e.tensor_add(
                                sb_qzb[64:128, ot * TSA:(ot + 1) * TSA],
                                eap[64:128, :], rap[64:128, :]),
                                reads=["ae", "ar"],
                                writes=[("aqB", ot)], inc="sG")
                        else:
                            P.op("gpsimd", lambda e, dst=dst, ot=ot, eap=eap, rap=rap: e.tensor_add(
                                dst[:, ot * TSA:(ot + 1) * TSA], eap, rap),
                                reads=["ae", "ar"], writes=[(dstk, ot)],
                                inc="sG")

                for fn_ in pending:
                    fn_()
                pending.clear()

                for c in (range(4) if "v" in ATTN_SECTIONS else ()):
                    for eh in range(2):
                        bkk = "pA" if (c * 2 + eh) % 2 == 0 else "pB"
                        mms = [lambda e, kt=kt, c=c, eh=eh, bkk=bkk: e.matmul(
                            BANK[bkk][:, 0:512],
                            sb_xbf[:, kt * TSA + c * 128:kt * TSA + (c + 1) * 128],
                            wsl(2, kt * D + eh * 512, kt * D + (eh + 1) * 512),
                            start=(kt == 0), stop=(kt == NF - 1))
                            for kt in range(NF)]
                        P.group("tensor", mms, reads=["axbf", ("w", 2)], writes=[bkk])
                        pv = BANK[bkk][:, 0:512].rearrange("p (h e) -> p h e", h=8)
                        bvv = sb_bvb[:, eh * 512:(eh + 1) * 512].rearrange(
                            "p (h e) -> p h e", h=8)
                        P.op("vector", lambda e, pv=pv, bvv=bvv, c=c, eh=eh: e.tensor_tensor(
                            v4[:, c, eh * 8:(eh + 1) * 8, 0:64], pv, bvv, ALU.add),
                            reads=[bkk, "bvb"], writes=[("av", c, eh)], inc="sV")

                for og in (range(4) if "ktr" in ATTN_SECTIONS else ()):
                    bkk = "pG" if og % 2 == 0 else "pH"
                    fns = []
                    for j in range(2):
                        ot = og * 2 + j
                        for c in range(4):
                            idx = j * 4 + c
                            fns.append(lambda e, ot=ot, c=c, idx=idx, bkk=bkk: e.transpose(
                                BANK[bkk][:, idx * 128:(idx + 1) * 128],
                                sb_kf[:, ot * TSA + c * 128:ot * TSA + (c + 1) * 128],
                                sb_eyebf[:]))
                    P.group("tensor", fns,
                            reads=[("akf", og * 2), ("akf", og * 2 + 1), "eyebf"],
                            writes=[bkk])
                    srcv = BANK[bkk][:, 0:1024].rearrange(
                        "p (j c h d) -> p j c h d", j=2, c=4, h=2)
                    P.op("scalar", lambda e, srcv=srcv, og=og: e.activation(
                        kta4[:, :, 2 * og:2 * og + 2, 0, :].rearrange(
                            "p c q d -> p q c d"),
                        srcv[:, :, :, 0, :], AF.Copy),
                        reads=[bkk], writes=[("aktA", og)], inc="sS")
                    P.op("scalar", lambda e, srcv=srcv, og=og: e.activation(
                        ktb4[:, :, 2 * og:2 * og + 2, 1, :].rearrange(
                            "p c q d -> p q c d"),
                        srcv[:, :, :, 1, :], AF.Copy),
                        reads=[bkk], writes=[("aktB", og)], inc="sS")

                for c in (range(4) if any(s.startswith("attn") for s in ATTN_SECTIONS) else ()):
                    gc = us * 4 + c
                    if gc == 0 and "attn_up" in ATTN_SECTIONS:
                        for sb_ in ("pE", "pF"):
                            P.op("tensor", lambda e, sb_=sb_: e.matmul(
                                BANK[sb_][:, 0:260], sb_zrow[0:1, 0:128],
                                sb_zrow[0:1, 0:260], start=True, stop=True),
                                reads=["zrow"], writes=[sb_], inc="sT")
                    ABANK = ("pA", "pB", "pC", "pD")
                    for g in (range(4) if "attn_sc" in ATTN_SECTIONS else ()):
                        bkk = ABANK[g]
                        mms = []
                        for j in range(4):
                            hh = 4 * g + j
                            pr, h2 = divmod(hh, 2)
                            qzsl = (sb_qza if h2 == 0 else sb_qzb)[
                                :, pr * TSA + c * 128:pr * TSA + (c + 1) * 128]
                            ksl = sb_kf[:, pr * TSA + c * 128:pr * TSA + (c + 1) * 128]
                            mms.append(lambda e, j=j, qzsl=qzsl, ksl=ksl, bkk=bkk: e.matmul(
                                BANK[bkk][:, j * 128:(j + 1) * 128],
                                ksl, qzsl, start=True, stop=True))
                        P.group("tensor", mms,
                                reads=[("aqA", 2 * g), ("aqA", 2 * g + 1),
                                       ("aqB", 2 * g), ("aqB", 2 * g + 1),
                                       ("akf", 2 * g), ("akf", 2 * g + 1)],
                                writes=[bkk])
                        if "nomask" in ATTN_SECTIONS:
                            continue
                        msk4 = sb_mask[:, 0:128].unsqueeze(1).broadcast_to((128, 4, 128))
                        P.op("vector", lambda e, g=g, bkk=bkk, msk4=msk4: e.tensor_tensor(
                            sb_sc[:, g * 512:(g + 1) * 512].rearrange(
                                "p (j t) -> p j t", j=4),
                            BANK[bkk][:, 0:512].rearrange("p (j t) -> p j t", j=4),
                            msk4, ALU.mult),
                            reads=[bkk, "mask"], writes=[("asc", g)], inc="sV")
                    for g in (range(4) if "attn_ae" in ATTN_SECTIONS else ()):
                        bkk = ABANK[g]
                        mms = []
                        rd = [("asc", g), "vones"]
                        for j2 in range(2):
                            pr = 2 * g + j2
                            rd.append(("av", c, pr // 4))
                            rd.append(("aqA", pr))
                            rd.append(("aqB", pr))
                            if gc > 0:
                                rd.append(("aSsb", pr // 4))
                            for h2 in range(2):
                                hh = 2 * pr + h2
                                base = j2 * 130 + h2 * 65
                                scl = sb_sc[:, hh * 128:(hh + 1) * 128]
                                vsl = v4[:, c, hh, 0:65]
                                mms.append(lambda e, base=base, scl=scl, vsl=vsl,
                                           bkk=bkk, last=(gc == 0): e.matmul(
                                    BANK[bkk][:, base:base + 65], scl, vsl,
                                    start=True, stop=last))
                                if gc > 0:
                                    qzsl = (sb_qza if h2 == 0 else sb_qzb)[
                                        :, pr * TSA + c * 128:pr * TSA + (c + 1) * 128]
                                    Ssl = sb_S[:, (pr // 4) * 260 + (pr % 4) * 65:
                                               (pr // 4) * 260 + (pr % 4) * 65 + 65]
                                    mms.append(lambda e, base=base, qzsl=qzsl,
                                               Ssl=Ssl, bkk=bkk: e.matmul(
                                        BANK[bkk][:, base:base + 65],
                                        qzsl, Ssl, start=False, stop=True))
                        P.group("tensor", mms, reads=rd, writes=[bkk])
                        dv = BANK[bkk][:, 0:260].rearrange("p (q e) -> p q e", e=65)
                        dslot = sb_den[:, g * 8:g * 8 + 4]
                        rslot = sb_den[:, g * 8 + 4:g * 8 + 8]
                        P.op("vector", lambda e, dv=dv, dslot=dslot: e.tensor_scalar_add(
                            dslot.unsqueeze(2), dv[:, :, 64:65], ATT_EPS),
                            reads=[bkk], writes=[("aden", g)], inc="sV")
                        P.op("vector", lambda e, dslot=dslot, rslot=rslot: e.reciprocal(
                            rslot, dslot),
                            reads=[("aden", g)], writes=[("arcp", g)], inc="sV")
                        P.op("vector", lambda e, dv=dv, g=g, rslot=rslot: e.tensor_tensor(
                            sb_att[:, g * 256:(g + 1) * 256].rearrange(
                                "p (q e) -> p q e", e=64),
                            dv[:, :, 0:64],
                            rslot.unsqueeze(2).broadcast_to((128, 4, 64)), ALU.mult),
                            reads=[bkk, ("arcp", g)], writes=[("aatt", g)], inc="sV")
                    if "attn_up" in ATTN_SECTIONS:
                        for sbank, prs in (("pE", range(0, 4)), ("pF", range(4, 8))):
                            mms = []
                            rd = [("aktA", 0), ("aktA", 1), ("aktA", 2), ("aktA", 3),
                                  ("aktB", 0), ("aktB", 1), ("aktB", 2), ("aktB", 3),
                                  ("av", c, 0), ("av", c, 1), "vones"]
                            for pr in prs:
                                o0 = (pr % 4) * 65
                                for h2 in range(2):
                                    hh = 2 * pr + h2
                                    ktz = sb_kta if h2 == 0 else sb_ktb
                                    ktsl = ktz[:, c * D + pr * 128:c * D + (pr + 1) * 128]
                                    vsl = v4[:, c, hh, 0:65]
                                    mms.append(lambda e, ktsl=ktsl, vsl=vsl, o0=o0,
                                               sbank=sbank: e.matmul(
                                        BANK[sbank][:, o0:o0 + 65],
                                        ktsl, vsl, start=False, stop=False,
                                        skip_group_check=True))
                            P.group("tensor", mms, reads=rd, writes=[sbank])
                        for bi, sbank in enumerate(("pE", "pF")):
                            P.op("vector", lambda e, bi=bi, sbank=sbank: e.tensor_copy(
                                sb_S[:, bi * 260:(bi + 1) * 260], BANK[sbank][:, 0:260]),
                                reads=[sbank], writes=[("aSsb", bi)], inc="sV")
                    if "attn_tr" not in ATTN_SECTIONS:
                        continue
                    bkk = "pG" if c % 2 == 0 else "pH"
                    fns = [lambda e, pr=pr, bkk=bkk: e.transpose(
                        BANK[bkk][:, pr * 128:(pr + 1) * 128],
                        sb_att[:, pr * 128:(pr + 1) * 128], sb_eyebf[:])
                        for pr in range(NP)]
                    P.group("tensor", fns,
                            reads=[("aatt", g) for g in range(4)] + ["eyebf"],
                            writes=[bkk])
                    srcv = BANK[bkk][:, 0:1024].rearrange("p (q t) -> p q t", q=8)
                    dstv = sb_attF[:].rearrange("p (q t) -> p q t", q=NF)[
                        :, :, c * 128:(c + 1) * 128]
                    P.op("scalar", lambda e, srcv=srcv, dstv=dstv: e.activation(
                        dstv, srcv, AF.Copy),
                         reads=[bkk], writes=[("aattF", c)], inc="sS")

                for ot in (range(NF) if "wo" in ATTN_SECTIONS else ()):
                    bkk = "pA" if ot % 2 == 0 else "pB"
                    mms = [lambda e, kt=kt, ot=ot, bkk=bkk: e.matmul(
                        BANK[bkk][:, 0:TSA],
                        wsl(3, kt * D + ot * 128, kt * D + (ot + 1) * 128),
                        sb_attF[:, kt * TSA:(kt + 1) * TSA],
                        start=(kt == 0), stop=(kt == NF - 1))
                        for kt in range(NF)]
                    P.group("tensor", mms,
                            reads=[("aattF", cc) for cc in range(4)] + [("w", 3)],
                            writes=[bkk])
                    P.op("vector", lambda e, ot=ot, bkk=bkk, s=src_f32: e.scalar_tensor_tensor(
                        sb_xa[:, ot * TSA:(ot + 1) * TSA], BANK[bkk][:, 0:TSA],
                        boc[ot], s[:, ot * TSA:(ot + 1) * TSA], ALU.add, ALU.add),
                        reads=[bkk, xk] + [("aqA", ot), ("aqB", ot)],
                        writes=["axa"], inc="sV")

                def mk_ln1(u=u, t0=t0, xk=xk):
                    if "ln1" in ATTN_SECTIONS:
                        emit_ln(TSA, NF, sb_xa[:], "axa", sb_cp[:],
                                sb_x[u % 2][:], xk, (g1c, be1c), "pC", "pD")
                    if "store" in ATTN_SECTIONS:
                        P.op("sync", lambda e, t0=t0, u=u: e.dma_start(
                            out=hbm(d_ln)[:, :, t0:t0 + TSA],
                            in_=sb_x[u % 2][:].rearrange("p (f t) -> p f t", f=NF)),
                            reads=[xk], writes=[("dln", u)], inc=f"st_a{u % 2}", n=16)
                mk_ln1()
            for fn_ in pending:
                fn_()
            pending.clear()

        def phase_b1(l):
            P.barrier_all()
            G = min(2, n_sup)
            A.reset(scratchB)
            sb_hst = [A.alloc(f"b1h{j}", [128, G * TSA], BF16) for j in range(4)]
            A.reset(wpool_off)
            sb_xl = [A.alloc("b1x0", [128, NF * TSA], F32),
                     A.alloc("b1x1", [128, NF * TSA], F32)]
            sb_xbf = A.alloc("b1xbf", [128, G * NF * TSA], BF16)
            assert A.off <= wpool_off + 4 * WREG * 2, "B1 scratch hits W1"
            P.op("gpsimd", lambda e: e.dma_start(
                out=wpool[:, 4 * WREG:4 * WREG + NF * DFF].rearrange(
                    "p (f d) -> p f d", f=NF), in_=hbm(w1[l])),
                writes=[("w", 4), ("w", 5), ("w", 6), ("w", 7)], inc="sW", n=16)
            P.level([("w", r) for r in range(4, 8)], "sW")
            b1c = [vcol(vl(l) + VC["b1"] + f) for f in range(NFF)]
            for g0 in range(0, n_sup, G):
                g = g0 // G
                Gg = min(G, n_sup - g0)
                for i in range(Gg):
                    u = g0 + i
                    t0 = u * TSA
                    P.op("sync", lambda e, t0=t0, u=u: e.dma_start(
                        out=sb_xl[u % 2][:].rearrange("p (f t) -> p f t", f=NF),
                        in_=hbm(d_ln)[:, :, t0:t0 + TSA]),
                        reads=[("dln", u)], writes=[("b1x", u % 2)],
                        inc=f"sxb{u % 2}", n=16)
                    P.op("vector", lambda e, i=i, u=u: e.tensor_copy(
                        sb_xbf[:, i * NF * TSA:(i + 1) * NF * TSA],
                        sb_xl[u % 2][:]),
                        reads=[("b1x", u % 2)], writes=[("b1bf", i)], inc="sV")
                for ht in range(NFF):
                    hp = ht % 4
                    HB = ("pA", "pB") if ht % 2 == 0 else ("pC", "pD")
                    mms = []
                    for kt in range(NF):
                        for i in range(Gg):
                            mms.append(lambda e, kt=kt, ht=ht, i=i, HB=HB: e.matmul(
                                BANK[HB[i]][:, 0:TSA],
                                wpool[:, 4 * WREG + kt * DFF + ht * 128:
                                      4 * WREG + kt * DFF + (ht + 1) * 128],
                                sb_xbf[:, i * NF * TSA + kt * TSA:
                                       i * NF * TSA + (kt + 1) * TSA],
                                start=(kt == 0), stop=(kt == NF - 1)))
                    P.group("tensor", mms,
                            reads=[("b1bf", i) for i in range(Gg)] +
                                  [("w", 4), ("w", 5), ("w", 6), ("w", 7)],
                            writes=list(HB[:Gg]))
                    for i in range(Gg):
                        P.op("scalar", lambda e, i=i, ht=ht, hp=hp, HB=HB: e.activation(
                            sb_hst[hp][:, i * TSA:(i + 1) * TSA],
                            BANK[HB[i]][:, 0:TSA],
                            AF.Gelu, bias=b1c[ht], scale=1.0),
                            reads=[HB[i]],
                            writes=[("b1h", hp)], inc="sS")
                    P.op("sync", lambda e, ht=ht, g0=g0, hp=hp, Gg=Gg: e.dma_start(
                        out=d_h[ht][:, g0 * TSA:(g0 + Gg) * TSA],
                        in_=sb_hst[hp][:, 0:Gg * TSA]),
                        reads=[("b1h", hp)], writes=[("dh", g, ht)],
                        inc=f"st_h{hp}", n=16)

        def phase_b2(l, d_dst, dst_tag, final_gf):
            P.barrier_all()
            A.reset(scratchB)
            sb_h = [nc.alloc_sbuf_tensor_at("b2h0", [128, NFF * TSA],
                                            BF16, offset=wpool_off + 4 * WREG * 2),
                    nc.alloc_sbuf_tensor_at("b2h1", [128, NFF * TSA],
                                            BF16, offset=wpool_off + 6 * WREG * 2)]
            sb_xl = [A.alloc("b2x0", [128, NF * TSA], F32),
                     A.alloc("b2x1", [128, NF * TSA], F32)]
            sb_cp2 = [A.alloc(f"b2cp{j}", [128, NF * TSA], BF16) for j in range(2)]
            rows2 = tuple(A.alloc(f"b2r{j}", [1, TSA], F32) for j in range(4)) + ("2",)
            P.op("gpsimd", lambda e: e.dma_start(
                out=wpool[:, 0:NFF * D].rearrange("p (f d) -> p f d", f=NFF),
                in_=hbm(w2[l])),
                writes=[("w", 0), ("w", 1), ("w", 2), ("w", 3)], inc="sW", n=16)
            P.level([("w", r) for r in range(4)], "sW")
            b2c = vcols(l, "b2")
            g2c, be2c = vcols(l, "g2"), vcols(l, "be2")
            gfc = [vcol(GF_C + f) for f in range(NF)]
            befc = [vcol(BEF_C + f) for f in range(NF)]
            pending = []
            for u in range(n_sup):
                t0 = u * TSA
                xk = ("b2x", u % 2)
                hk = ("b2h", u % 2)
                P.op("sync", lambda e, t0=t0, u=u: e.dma_start(
                    out=sb_xl[u % 2][:].rearrange("p (f t) -> p f t", f=NF),
                    in_=hbm(d_ln)[:, :, t0:t0 + TSA]),
                    reads=[("dln", u)], writes=[xk], inc=f"sxb{u % 2}", n=16)
                for hf in range(2):
                    P.op("sync", lambda e, t0=t0, u=u, hf=hf: e.dma_start(
                        out=sb_h[u % 2][:, hf * 16 * TSA:(hf + 1) * 16 * TSA]
                        .rearrange("p (k t) -> p k t", k=16),
                        in_=d_h.rearrange("k p t -> p k t")[
                            :, hf * 16:(hf + 1) * 16, t0:t0 + TSA]),
                        writes=[(hk, hf)], inc=f"sh{u % 2}{hf}", n=16)
                for ot in range(NF):
                    bkk = ("pA", "pB", "pE", "pF")[ot % 4]
                    for hf in range(2):
                        mms = [lambda e, kt=kt, ot=ot, bkk=bkk, u=u: e.matmul(
                            BANK[bkk][:, 0:TSA],
                            wpool[:, kt * D + ot * 128:kt * D + (ot + 1) * 128],
                            sb_h[u % 2][:, kt * TSA:(kt + 1) * TSA],
                            start=(kt == 0), stop=(kt == NFF - 1))
                            for kt in range(hf * 16, (hf + 1) * 16)]
                        P.group("tensor", mms,
                                reads=[(hk, hf), ("w", 0), ("w", 1), ("w", 2), ("w", 3)],
                                writes=[bkk])
                    P.op("vector", lambda e, ot=ot, bkk=bkk, u=u: e.scalar_tensor_tensor(
                        sb_xl[u % 2][:, ot * TSA:(ot + 1) * TSA], BANK[bkk][:, 0:TSA],
                        b2c[ot], sb_xl[u % 2][:, ot * TSA:(ot + 1) * TSA],
                        ALU.add, ALU.add),
                        reads=[bkk, xk], writes=[xk], inc="sV")
                    if pending:
                        pending.pop(0)()
                    if final_gf and pending:
                        pending.pop(0)()
                def mk_ln2(u=u, t0=t0, xk=xk):
                    st = []
                    rws = None if u % 2 == 0 else rows2
                    cpp = sb_cp2[u % 2][:]
                    emit_ln(TSA, NF, sb_xl[u % 2][:], xk, cpp,
                            sb_xl[u % 2][:], xk, (g2c, be2c), "pC", "pD",
                            rows=rws, defer=st)
                    if final_gf:
                        emit_ln(TSA, NF, sb_xl[u % 2][:], xk, cpp,
                                sb_xl[u % 2][:], xk, (gfc, befc), "pC", "pD",
                                rows=rws, defer=st)

                    def store():
                        P.op("sync", lambda e, t0=t0, u=u: e.dma_start(
                            out=hbm(d_dst)[:, :, t0:t0 + TSA],
                            in_=sb_xl[u % 2][:].rearrange("p (f t) -> p f t", f=NF)),
                            reads=[xk], writes=[(dst_tag, u)], inc=f"st_b{u % 2}", n=16)
                    st.append(store)
                    return st
                pending.extend(mk_ln2())
            while pending:
                pending.pop(0)()

        def phase_c():
            P.barrier_all()
            A.reset(scratchC)
            sb_am = [A.alloc(f"cam{j}", [128, NF * TSM], F32) for j in range(2)]
            sb_xi = [A.alloc(f"cxi{j}", [128, NF * TSM], F32) for j in range(2)]
            sb_xm = A.alloc("cxm", [128, NF * TSM], BF16)
            sb_cp2 = [A.alloc(f"ccp{j}", [128, NF * TSM], BF16) for j in range(2)]
            sb_pr = A.alloc("cpr", [128, TSM], F32)
            sb_tm = A.alloc("ctm", [128, 24], F32)
            sb_t2 = A.alloc("ct2", [128, 24], F32)
            sb_p1 = A.alloc("cp1", [128, 4], F32)
            sb_p2 = A.alloc("cp2", [128, 4], F32)
            sb_sg = A.alloc("csg", [128, 4], F32)
            sb_ger = A.alloc("cger", [6, TSM], F32)
            rows2 = tuple(A.alloc(f"cr{j}", [1, TSM], F32) for j in range(4)) + ("2",)

            P.op("gpsimd", lambda e: e.dma_start(
                out=wsl(0, 0, NF * NE).rearrange("p (f d) -> p f d", f=NF),
                in_=hbm(gwt)), writes=[("w", 0)], inc="sW", n=16)
            P.level([("w", 0)], "sW")
            gbcol = vcol(GB_C)

            for u in range(nm):
                t0 = u * TSM
                amk = ("cam", u % 2)
                xik = ("cxi", u % 2)
                P.op("sync", lambda e, t0=t0, u=u: e.dma_start(
                    out=sb_am[u % 2][:].rearrange("p (f t) -> p f t", f=NF),
                    in_=hbm(d_b)[:, :, t0:t0 + TSM]),
                    reads=[("dB", u)], writes=[amk], inc=f"sca{u % 2}", n=16)
                P.op("sync", lambda e, t0=t0, u=u: e.dma_start(
                    out=sb_xi[u % 2][:].rearrange("p (f t) -> p f t", f=NF),
                    in_=hbm(x_in)[:, :, t0:t0 + TSM]),
                    writes=[xik], inc=f"sci{u % 2}", n=16)
                sbk, bck = ("pA", "pB") if u % 2 == 0 else ("pE", "pF")
                rws = None if u % 2 == 0 else rows2
                emit_ln(TSM, NF, sb_am[u % 2][:], amk, sb_cp2[u % 2][:],
                        sb_am[u % 2][:], amk, None, sbk, bck, rows=rws)
                P.op("vector", lambda e, u=u: e.tensor_tensor(
                    sb_xi[u % 2][:], sb_xi[u % 2][:], sb_am[u % 2][:], ALU.add),
                    reads=[amk, xik], writes=[xik], inc="sV")
                P.op("sync", lambda e, t0=t0, u=u: e.dma_start(
                    out=hbm(d_a)[:, :, t0:t0 + TSM],
                    in_=sb_xi[u % 2][:].rearrange("p (f t) -> p f t", f=NF)),
                    reads=[xik], writes=[("dA", u)], inc=f"st_c5{u % 2}", n=16)
                emit_ln(TSM, NF, sb_xi[u % 2][:], xik, sb_cp2[u % 2][:],
                        sb_am[u % 2][:], amk, None, sbk, bck,
                        bf16_dst=sb_xm[:], bf16_key="cxm", rows=rws)
                P.op("sync", lambda e, t0=t0: e.dma_start(
                    out=hbm(d_xm)[:, :, t0:t0 + TSM],
                    in_=sb_xm[:].rearrange("p (f t) -> p f t", f=NF)),
                    reads=["cxm"], writes=[("dxm", u)], inc="st_cxm", n=16)
                mms = [lambda e, kt=kt: e.matmul(
                    BANK["pC"][0:NE, 0:TSM], wsl(0, kt * NE, (kt + 1) * NE),
                    sb_xm[:, kt * TSM:(kt + 1) * TSM],
                    start=(kt == 0), stop=(kt == NF - 1)) for kt in range(NF)]
                P.group("tensor", mms, reads=["cxm", ("w", 0)], writes=["pC"])
                P.op("scalar", lambda e: e.activation(
                    sb_pr[0:NE, 0:TSM], BANK["pC"][0:NE, 0:TSM], AF.Sigmoid,
                    bias=gbcol[0:NE, :], scale=1.0),
                    reads=["pC"], writes=["cpr"], inc="sS")
                fns = [lambda e, cc=cc: e.transpose(
                    BANK["pD"][:, cc * NE:(cc + 1) * NE],
                    sb_pr[0:NE, cc * 128:(cc + 1) * 128], sb_eye32[0:NE, 0:NE])
                    for cc in range(4)]
                P.group("tensor", fns, reads=["cpr", "eye32"], writes=["pD"])
                P.op("vector", lambda e: e.tensor_copy(sb_tm[:], BANK["pD"][:, 0:24]),
                     reads=["pD"], writes=["ctm"], inc="sV")
                tm3 = sb_tm[:].rearrange("p (c e) -> p c e", c=4)
                t23 = sb_t2[:].rearrange("p (c e) -> p c e", c=4)
                P.op("vector", lambda e, tm3=tm3: e.tensor_reduce(
                    sb_p1[:].unsqueeze(2), tm3, AX.X, ALU.max),
                    reads=["ctm"], writes=["cp1"], inc="sV")
                P.op("vector", lambda e, tm3=tm3, t23=t23: e.tensor_tensor(
                    t23, tm3, sb_p1[:].unsqueeze(2).broadcast_to((128, 4, NE)),
                    ALU.is_ge), reads=["ctm", "cp1"], writes=["ct2"], inc="sV")
                P.op("vector", lambda e: e.scalar_tensor_tensor(
                    sb_t2[:], sb_t2[:], -10.0, sb_tm[:], ALU.mult, ALU.add),
                    reads=["ct2", "ctm"], writes=["ct2"], inc="sV")
                P.op("vector", lambda e, t23=t23: e.tensor_reduce(
                    sb_p2[:].unsqueeze(2), t23, AX.X, ALU.max),
                    reads=["ct2"], writes=["cp2"], inc="sV")
                P.op("vector", lambda e: e.tensor_tensor(
                    sb_sg[:], sb_p1[:], sb_p2[:], ALU.add),
                    reads=["cp1", "cp2"], writes=["csg"], inc="sV")
                P.op("vector", lambda e: e.tensor_scalar_add(
                    sb_sg[:], sb_sg[:], GATE_EPS),
                    reads=["csg"], writes=["csg"], inc="sV")
                P.op("vector", lambda e: e.reciprocal(sb_sg[:], sb_sg[:]),
                     reads=["csg"], writes=["csg"], inc="sV")
                P.op("vector", lambda e, tm3=tm3, t23=t23: e.tensor_tensor(
                    t23, tm3, sb_p2[:].unsqueeze(2).broadcast_to((128, 4, NE)),
                    ALU.is_ge), reads=["ctm", "cp2"], writes=["ct2"], inc="sV")
                P.op("vector", lambda e: e.tensor_tensor(
                    sb_t2[:], sb_t2[:], sb_tm[:], ALU.mult),
                    reads=["ct2", "ctm"], writes=["ct2"], inc="sV")
                P.op("vector", lambda e, t23=t23: e.tensor_tensor(
                    t23, t23, sb_sg[:].unsqueeze(2).broadcast_to((128, 4, NE)),
                    ALU.mult), reads=["ct2", "csg"], writes=["ct2"], inc="sV")
                fns = [lambda e, cc=cc: e.transpose(
                    BANK["pD"][0:NE, cc * 128:(cc + 1) * 128],
                    sb_t2[:, cc * NE:(cc + 1) * NE], sb_eye32[:, 0:128])
                    for cc in range(4)]
                P.group("tensor", fns, reads=["ct2", "eye32"], writes=["pD"])
                P.op("vector", lambda e: e.tensor_copy(
                    sb_ger[:], BANK["pD"][0:NE, 0:TSM]),
                    reads=["pD"], writes=["cger"], inc="sV")
                P.op("sync", lambda e, t0=t0: e.dma_start(
                    out=d_ge[:, t0:t0 + TSM], in_=sb_ger[:]),
                    reads=["cger"], writes=[("dge", u)], inc="st_cge", n=16)

        def phase_expert(e_):
            P.barrier_all()
            A.reset(scratchE)
            G = min(2, nm)
            sb_xmb = [A.alloc("ex0", [128, NF * TSM], BF16),
                      A.alloc("ex1", [128, NF * TSM], BF16)]
            sb_s1 = A.alloc("es1", [128, 2 * TSM], F32)
            sb_he = A.alloc("ehe", [128, G * 8 * TSM], BF16)
            sb_geb = A.alloc("egeb", [128, G * TSM], F32)
            sb_ger = A.alloc("eger", [1, n_tok], F32)
            sb_y = [A.alloc("ey0", [128, NF * TSM], F32),
                    A.alloc("ey1", [128, NF * TSM], F32)]

            shared = e_ < 0
            if shared:
                srcs = (fc1t, fc2t, fc3t)
            else:
                srcs = (ew1t[e_], ew3t[e_], ew2t[e_])
            for reg, w_ in enumerate(srcs):
                P.op("gpsimd", lambda e, reg=reg, w_=w_: e.dma_start(
                    out=wsl(reg, 0, NF * 1024).rearrange("p (f d) -> p f d", f=NF),
                    in_=hbm(w_)), writes=[("w", reg)], inc="sW", n=16)
            P.level([("w", r) for r in range(3)], "sW")
            if not shared:
                P.op("sync", lambda e: e.dma_start(out=sb_ger[:], in_=d_ge[e_:e_ + 1, :]),
                     reads=[("dge", uu) for uu in range(nm)], writes=["eger"],
                     inc="sge", n=16)

            for g0 in range(0, nm, G):
                Gg = min(G, nm - g0)
                for i in range(Gg):
                    u = g0 + i
                    t0 = u * TSM
                    P.op("sync", lambda e, t0=t0, i=i: e.dma_start(
                        out=sb_xmb[i][:].rearrange("p (f t) -> p f t", f=NF),
                        in_=hbm(d_xm)[:, :, t0:t0 + TSM]),
                        reads=[("dxm", u)], writes=[("exm", i)],
                        inc=f"sxe{i}", n=16)
                    if not shared:
                        P.op("tensor", lambda e, t0=t0, i=i: e.matmul(
                            BANK[("pE", "pF")[i]][:, 0:TSM], sb_rows[0:1, 0:128],
                            sb_ger[0:1, t0:t0 + TSM], start=True, stop=True),
                            reads=["eger", "rows"], writes=[("pE", "pF")[i]], inc="sT")
                        P.op("vector", lambda e, i=i: e.tensor_copy(
                            sb_geb[:, i * TSM:(i + 1) * TSM],
                            BANK[("pE", "pF")[i]][:, 0:TSM]),
                            reads=[("pE", "pF")[i]], writes=[("egeb", i)], inc="sV")
                for ht in range(8):
                    for reg, banks in ((0, ("pA", "pB")), (1, ("pC", "pD"))):
                        mms = []
                        for kt in range(NF):
                            for i in range(Gg):
                                mms.append(lambda e, kt=kt, ht=ht, reg=reg, i=i,
                                           banks=banks: e.matmul(
                                    BANK[banks[i]][:, 0:TSM],
                                    wsl(reg, kt * 1024 + ht * 128,
                                        kt * 1024 + (ht + 1) * 128),
                                    sb_xmb[i][:, kt * TSM:(kt + 1) * TSM],
                                    start=(kt == 0), stop=(kt == NF - 1)))
                        P.group("tensor", mms,
                                reads=[("exm", 0), ("exm", 1), ("w", reg)],
                                writes=list(banks[:Gg]))
                    for i in range(Gg):
                        b1k = ("pA", "pB")[i]
                        b3k = ("pC", "pD")[i]
                        he_sl = sb_he[:, (i * 8 + ht) * TSM:(i * 8 + ht + 1) * TSM]
                        s1 = sb_s1[:, i * TSM:(i + 1) * TSM]
                        P.op("scalar", lambda e, b1k=b1k, s1=s1: e.activation(
                            s1, BANK[b1k][:, 0:TSM], AF.Silu),
                            reads=[b1k], writes=[("es1", i)], inc="sS")
                        if shared:
                            P.op("vector", lambda e, he_sl=he_sl, b3k=b3k, s1=s1: e.tensor_tensor(
                                he_sl, s1, BANK[b3k][:, 0:TSM], ALU.mult),
                                reads=[("es1", i), b3k], writes=[("ehe", i, ht)], inc="sV")
                        else:
                            P.op("vector", lambda e, b3k=b3k, s1=s1: e.tensor_tensor(
                                s1, s1, BANK[b3k][:, 0:TSM], ALU.mult),
                                reads=[("es1", i), b3k], writes=[("es1", i)], inc="sV")
                            P.op("vector", lambda e, he_sl=he_sl, i=i, s1=s1: e.tensor_tensor(
                                he_sl, s1,
                                sb_geb[:, i * TSM:(i + 1) * TSM], ALU.mult),
                                reads=[("es1", i), ("egeb", i)], writes=[("ehe", i, ht)],
                                inc="sV")
                for ot in range(NF):
                    banks = ("pE", "pF") if ot % 2 == 0 else ("pA", "pB")
                    mms = []
                    for kt in range(8):
                        for i in range(Gg):
                            mms.append(lambda e, kt=kt, ot=ot, i=i, banks=banks: e.matmul(
                                BANK[banks[i]][:, 0:TSM],
                                wsl(2, kt * D + ot * 128, kt * D + (ot + 1) * 128),
                                sb_he[:, (i * 8 + kt) * TSM:(i * 8 + kt + 1) * TSM],
                                start=(kt == 0), stop=(kt == 7)))
                    P.group("tensor", mms,
                            reads=[("ehe", i, kt) for i in range(Gg) for kt in range(8)] +
                                  [("w", 2)],
                            writes=[banks[0], banks[1]])
                    for i in range(Gg):
                        P.op("scalar", lambda e, ot=ot, i=i, banks=banks: e.activation(
                            sb_y[i][:, ot * TSM:(ot + 1) * TSM],
                            BANK[banks[i]][:, 0:TSM], AF.Copy),
                            reads=[banks[i]], writes=[("ey", i)], inc="sS")
                for i in range(Gg):
                    u = g0 + i
                    t0 = u * TSM
                    P.op("gpsimd", lambda e, t0=t0, i=i, shared=shared: e.dma_start(
                        out=hbm(d_y)[:, :, t0:t0 + TSM],
                        in_=sb_y[i][:].rearrange("p (f t) -> p f t", f=NF),
                        accum_op=(ALU.bypass if shared else ALU.add)),
                        reads=[("ey", i)] + ([] if shared else [("dy", u)]),
                        writes=[("dy", u)], inc=f"st_y{i}", n=16)

        def phase_final():
            P.barrier_all()
            A.reset(scratchC)
            sb_y = [A.alloc(f"fy{j}", [128, NF * TSM], F32) for j in range(2)]
            sb_x5 = [A.alloc(f"fx{j}", [128, NF * TSM], F32) for j in range(2)]
            sb_cp2 = [A.alloc(f"fcp{j}", [128, NF * TSM], BF16) for j in range(2)]
            rows2 = tuple(A.alloc(f"fr{j}", [1, TSM], F32) for j in range(4)) + ("2",)
            for u in range(nm):
                t0 = u * TSM
                yk = ("fy", u % 2)
                xk = ("fx", u % 2)
                P.op("sync", lambda e, t0=t0, u=u: e.dma_start(
                    out=sb_y[u % 2][:].rearrange("p (f t) -> p f t", f=NF),
                    in_=hbm(d_y)[:, :, t0:t0 + TSM]),
                    reads=[("dy", u)], writes=[yk], inc=f"sfy{u % 2}", n=16)
                P.op("sync", lambda e, t0=t0, u=u: e.dma_start(
                    out=sb_x5[u % 2][:].rearrange("p (f t) -> p f t", f=NF),
                    in_=hbm(d_a)[:, :, t0:t0 + TSM]),
                    reads=[("dA", u)], writes=[xk], inc=f"sfx{u % 2}", n=16)
                sbk, bck = ("pA", "pB") if u % 2 == 0 else ("pC", "pD")
                rws = None if u % 2 == 0 else rows2
                emit_ln(TSM, NF, sb_y[u % 2][:], yk, sb_cp2[u % 2][:],
                        sb_y[u % 2][:], yk, None, sbk, bck, rows=rws)
                P.op("vector", lambda e, u=u: e.tensor_tensor(
                    sb_x5[u % 2][:], sb_x5[u % 2][:], sb_y[u % 2][:], ALU.add),
                    reads=[yk, xk], writes=[xk], inc="sV")
                P.op("sync", lambda e, t0=t0, u=u: e.dma_start(
                    out=hbm(out)[:, :, t0:t0 + TSM],
                    in_=sb_x5[u % 2][:].rearrange("p (f t) -> p f t", f=NF)),
                    reads=[xk], inc=f"st_f{u % 2}", n=16)

        phases = [
            lambda: phase_attn(0, x_in, "dx", True),
            lambda: phase_b1(0), lambda: phase_b2(0, d_a, "dA", False),
            lambda: phase_attn(1, d_a, "dA", False),
            lambda: phase_b1(1), lambda: phase_b2(1, d_b, "dB", False),
            lambda: phase_attn(2, d_b, "dB", False),
            lambda: phase_b1(2), lambda: phase_b2(2, d_a, "dA", False),
            lambda: phase_attn(3, d_a, "dA", False),
            lambda: phase_b1(3), lambda: phase_b2(3, d_b, "dB", True),
            phase_c,
            lambda: phase_expert(-1),
        ] + [(lambda e_=e_: phase_expert(e_)) for e_ in range(NE)] + [phase_final]
        for ph in phases[:upto]:
            ph()

        for s_ in [s for s in P.cnt if s.startswith("st_")]:
            eng_ = "gpsimd" if s_.startswith("st_y") else "sync"
            P.q[eng_].append(lambda e, s_=s_, v_=P.cnt[s_]: e.wait_ge(P.sems[s_], v_))

        block = es.enter_context(nc.Block())

        def _skip_redundant_ldweights():
            import bass_rust
            n_skip = 0
            for f_ in nc.m.functions:
                for blk_ in f_.blocks:
                    il = blk_.instructions
                    prev_key = None
                    changed = False
                    for idx, ins_ in enumerate(il):
                        if isinstance(ins_, mybir.InstMatmult):
                            if ins_.is_transpose:
                                prev_key = None
                                continue
                            try:
                                wop = ins_.ins[1]
                                key = (str(wop), str(ins_.tile_position),
                                       str(ins_.perf_mode))
                            except Exception:
                                prev_key = None
                                continue
                            if "float32" in key[0]:
                                prev_key = None
                                continue
                            if prev_key is not None and key == prev_key:
                                il[idx] = ins_.__replace__(ldweights=False)
                                changed = True
                                n_skip += 1
                            prev_key = key
                        elif not isinstance(ins_, (mybir.InstEventSemaphore,)):
                            if type(ins_).__name__ not in (
                                    "InstEventSemaphore", "InstNop"):
                                prev_key = None
                    if changed:
                        blk_.instructions = il
            return n_skip

        @block.sync
        def _(eng):
            for fn in P.q["sync"]:
                fn(eng)

        @block.gpsimd
        def _(eng):
            for fn in P.q["gpsimd"]:
                fn(eng)

        @block.tensor
        def _(eng):
            for fn in P.q["tensor"]:
                fn(eng)

        @block.vector
        def _(eng):
            for fn in P.q["vector"]:
                fn(eng)

        @block.scalar
        def _(eng):
            for fn in P.q["scalar"]:
                fn(eng)

        if SKIP_LDW:
            _skip_redundant_ldweights()

    return nc


_CACHE = {}


def _get_prog(n_tok, n_seq, dbg=False):
    key = (n_tok, n_seq, dbg)
    if key not in _CACHE:
        _CACHE[key] = build_program(n_tok, n_seq, dbg)
    return _CACHE[key]


def _pack_cols(vec):
    v = np.asarray(vec, np.float32)
    return np.ascontiguousarray(v.reshape(-1, 128).T)


def host_inputs(kw):
    f32 = np.float32
    base = {}
    for l in range(NL):
        base[f"wq{l}"] = np.ascontiguousarray(np.asarray(kw["L0_attn_Wq"][l], f32)).reshape(NF, 128, D)
        base[f"wk{l}"] = np.ascontiguousarray(np.asarray(kw["L0_attn_Wk"][l], f32)).reshape(NF, 128, D)
        base[f"wv{l}"] = np.ascontiguousarray(np.asarray(kw["L0_attn_Wv"][l], f32)).reshape(NF, 128, D)
        base[f"wo{l}"] = np.ascontiguousarray(np.asarray(kw["L0_attn_Wo"][l], f32)).reshape(NF, 128, D)
        base[f"w1_{l}"] = np.ascontiguousarray(np.asarray(kw["L0_attn_W1"][l], f32)).reshape(NF, 128, DFF)
        base[f"w2_{l}"] = np.ascontiguousarray(np.asarray(kw["L0_attn_W2"][l], f32)).reshape(NFF, 128, D)
    vecs = np.zeros((128, 416), f32)
    for l in range(NL):
        b = 96 * l
        vecs[:, b + 0:b + 8] = _pack_cols(kw["L0_attn_bq"][l])
        vecs[:, b + 8:b + 16] = _pack_cols(kw["L0_attn_bk"][l])
        vecs[:, b + 16:b + 24] = _pack_cols(kw["L0_attn_bo"][l])
        vecs[:, b + 24:b + 56] = _pack_cols(kw["L0_attn_b1"][l])
        vecs[:, b + 56:b + 64] = _pack_cols(kw["L0_attn_b2"][l])
        vecs[:, b + 64:b + 72] = _pack_cols(kw["L0_attn_g1"][l])
        vecs[:, b + 72:b + 80] = _pack_cols(kw["L0_attn_be1"][l])
        vecs[:, b + 80:b + 88] = _pack_cols(kw["L0_attn_g2"][l])
        vecs[:, b + 88:b + 96] = _pack_cols(kw["L0_attn_be2"][l])
    vecs[:, 384:392] = _pack_cols(kw["L0_attn_gf"])
    vecs[:, 392:400] = _pack_cols(kw["L0_attn_bef"])
    gb = np.zeros(128, f32)
    gb[:NE] = np.asarray(kw["L0_ffn_gate_b"], f32)
    vecs[:, 400] = gb
    base["vecs"] = vecs
    rows = np.zeros((1, 128 + NL * D), f32)
    rows[0, :128] = 1.0
    for l in range(NL):
        rows[0, 128 + l * D:128 + (l + 1) * D] = np.asarray(kw["L0_attn_bv"][l], f32)
    base["rows"] = rows
    base["maskT"] = np.triu(np.ones((128, 128), f32))
    base["eye"] = np.eye(128, dtype=f32)
    base["onec"] = np.ones((128, 1), f32)
    base["ew1t"] = np.ascontiguousarray(np.asarray(kw["L0_ffn_ew1"], f32).transpose(0, 2, 1)).reshape(NE, NF, 128, MOE_I)
    base["ew3t"] = np.ascontiguousarray(np.asarray(kw["L0_ffn_ew3"], f32).transpose(0, 2, 1)).reshape(NE, NF, 128, MOE_I)
    base["ew2t"] = np.ascontiguousarray(np.asarray(kw["L0_ffn_ew2"], f32).transpose(0, 2, 1)).reshape(NE, NF, 128, D)
    base["fc1t"] = np.ascontiguousarray(np.asarray(kw["L0_ffn_fc1"], f32).T).reshape(NF, 128, MOE_I)
    base["fc2t"] = np.ascontiguousarray(np.asarray(kw["L0_ffn_fc2"], f32).T).reshape(NF, 128, MOE_I)
    base["fc3t"] = np.ascontiguousarray(np.asarray(kw["L0_ffn_fc3"], f32).T).reshape(NF, 128, D)
    base["gwt"] = np.ascontiguousarray(np.asarray(kw["L0_ffn_gate_w"], f32).T).reshape(NF, 128, NE)
    return base


def kernel(**kw):
    n_cores = 8
    x = np.asarray(kw["L0_attn_x"], np.float32)
    B, S, _ = x.shape
    per = B // n_cores
    n_tok = per * S
    base = host_inputs(kw)
    nc = _get_prog(n_tok, per)
    in_maps = []
    for c in range(n_cores):
        xc = np.ascontiguousarray(
            x[c * per:(c + 1) * per].reshape(n_tok, D).T).reshape(NF, 128, n_tok)
        in_maps.append({**base, "x": xc})
    res = run_bass_kernel_spmd(nc, in_maps, core_ids=list(range(n_cores)))
    outs = []
    for c in range(n_cores):
        o = np.asarray(res.results[c]["out"]).reshape(D, n_tok)
        outs.append(np.ascontiguousarray(o.T).reshape(per, S, D))
    return np.concatenate(outs, axis=0).astype(np.float32)


# version 16
# speedup vs baseline: 1.0370x; 1.0370x over previous
import numpy as np
import ml_dtypes
from contextlib import ExitStack

import concourse.bass as bass
import concourse.mybir as mybir
from concourse.bass_utils import run_bass_kernel_spmd

F32 = mybir.dt.float32
BF16 = mybir.dt.bfloat16
AF = mybir.ActivationFunctionType
ALU = mybir.AluOpType
AX = mybir.AxisListType

D = 1024
NF = 8
NH = 16
HD = 64
NP = 8
CH = 128
DFF = 4096
NFF = 32
NE = 6
NL = 4
MOE_I = 1024
LN_EPS = 1e-5
ATT_EPS = 1e-6
GATE_EPS = 1e-8

TSA = 512
TSB = 256
TSM = 512
SKIP_LDW = True
ATTN_SECTIONS = {"preln", "qk", "v", "ktr", "attn_sc", "attn_ae", "attn_up", "attn_tr", "wo", "ln1", "store"}

ENGINES = ("sync", "gpsimd", "tensor", "vector", "scalar")


class Buf:
    __slots__ = ("w", "r")

    def __init__(self):
        self.w = None
        self.r = []

    def rd(self):
        return [self.w] if self.w else []

    def wr(self):
        d = list(self.r)
        if self.w:
            d.append(self.w)
        return d


class Prog:
    def __init__(self, nc, es):
        self.nc = nc
        self.es = es
        self.q = {e: [] for e in ENGINES}
        self.sems = {}
        self.cnt = {}
        self.est = {}
        self.bufs = {}
        self.pending = {}
        for s in ("sT", "sV", "sS", "sG"):
            self.sem(s)

    def sem(self, s):
        if s not in self.sems:
            self.sems[s] = self.es.enter_context(self.nc.semaphore(s))
            self.cnt[s] = 0
        return s

    def level(self, keys, s):
        tok = (s, self.cnt[s])
        for k in keys:
            self.buf(k).w = tok

    def buf(self, key):
        b = self.bufs.get(key)
        if b is None:
            b = self.bufs[key] = Buf()
        return b

    def _emit(self, eng, fn, dep, inc, n):
        if inc is not None:
            self.sem(inc)
        waits = []
        for t in dep:
            if t is None:
                continue
            s, v = t
            if self.est.get((eng, s), 0) < v:
                waits.append((s, v))
                self.est[(eng, s)] = v
        tok = None
        if inc is not None:
            self.sem(inc)
            self.cnt[inc] += n
            tok = (inc, self.cnt[inc])
        sems = self.sems

        def run(e, fn=fn, waits=tuple(waits), tok=tok, n=n):
            for s, v in waits:
                e.wait_ge(sems[s], v)
            ins = fn(e)
            if tok is not None:
                ins.then_inc(sems[tok[0]], n)

        self.q[eng].append(run)
        return tok

    def op(self, eng, fn, reads=(), writes=(), deps=(), inc=None, n=1):
        dep = list(deps) + list(self.pending.pop(eng, ()))
        for k in reads:
            dep += self.buf(k).rd()
        for k in writes:
            dep += self.buf(k).wr()
        tok = self._emit(eng, fn, dep, inc, n)
        for k in reads:
            if tok is not None:
                self.buf(k).r.append(tok)
        for k in writes:
            b = self.buf(k)
            b.w = tok
            b.r = []
        return tok

    def group(self, eng, fns, reads=(), writes=(), deps=(), inc="sT", n=1):
        assert fns
        if len(fns) == 1:
            return self.op(eng, fns[0], reads=reads, writes=writes, deps=deps,
                           inc=inc, n=n)
        dep = list(deps) + list(self.pending.pop(eng, ()))
        for k in reads:
            dep += self.buf(k).rd()
        for k in writes:
            dep += self.buf(k).wr()
        self._emit(eng, fns[0], dep, None, 1)
        for fn in fns[1:-1]:
            self._emit(eng, fn, (), None, 1)
        tok = self._emit(eng, fns[-1], (), inc, n)
        for k in reads:
            if tok is not None:
                self.buf(k).r.append(tok)
        for k in writes:
            b = self.buf(k)
            b.w = tok
            b.r = []
        return tok

    def barrier_all(self):
        toks = [(s, c) for s, c in self.cnt.items() if c > 0]
        for e in ENGINES:
            self.pending[e] = list(toks)


class Arena:
    def __init__(self, nc, base, top):
        self.nc = nc
        self.base = (base + 31) // 32 * 32
        self.top = top
        self.off = self.base
        self.uid = 0

    def alloc(self, name, shape, dtype):
        self.off = (self.off + 31) // 32 * 32
        t = self.nc.alloc_sbuf_tensor_at(f"{name}_{self.uid}", shape, dtype,
                                         offset=self.off)
        self.uid += 1
        self.off += int(np.prod(shape[1:])) * mybir.dt.size(dtype)
        assert self.off <= self.top, (
            f"SBUF overflow at {name}: used {self.off - self.base} of "
            f"{self.top - self.base}")
        return t

    def alloc_at(self, name, shape, dtype, off):
        t = self.nc.alloc_sbuf_tensor_at(f"{name}_{self.uid}", shape, dtype,
                                         offset=(off + 31) // 32 * 32)
        self.uid += 1
        return t

    def reset(self, off):
        self.off = off


def build_program(n_tok, n_seq, dbg=False, upto=99):
    seq = n_tok // n_seq
    assert seq % TSA == 0 and n_tok % TSB == 0 and n_tok % TSM == 0
    n_sup = n_tok // TSA
    sup_seq = seq // TSA
    nb = n_tok // TSB
    nm = n_tok // TSM

    nc = bass.Bass("TRN2", target_bir_lowering=False)

    x_in = nc.dram_tensor("x", [NF, 128, n_tok], F32, kind="ExternalInput").ap()
    wq, wk, wv, wo, w1, w2 = [], [], [], [], [], []
    for l in range(NL):
        wq.append(nc.dram_tensor(f"wq{l}", [NF, 128, D], F32, kind="ExternalInput").ap())
        wk.append(nc.dram_tensor(f"wk{l}", [NF, 128, D], F32, kind="ExternalInput").ap())
        wv.append(nc.dram_tensor(f"wv{l}", [NF, 128, D], F32, kind="ExternalInput").ap())
        wo.append(nc.dram_tensor(f"wo{l}", [NF, 128, D], F32, kind="ExternalInput").ap())
        w1.append(nc.dram_tensor(f"w1_{l}", [NF, 128, DFF], F32, kind="ExternalInput").ap())
        w2.append(nc.dram_tensor(f"w2_{l}", [NFF, 128, D], F32, kind="ExternalInput").ap())
    vecs_in = nc.dram_tensor("vecs", [128, 416], F32, kind="ExternalInput").ap()
    rows_in = nc.dram_tensor("rows", [1, 128 + NL * D], F32, kind="ExternalInput").ap()
    mask_in = nc.dram_tensor("maskT", [128, 128], F32, kind="ExternalInput").ap()
    eye_in = nc.dram_tensor("eye", [128, 128], F32, kind="ExternalInput").ap()
    onec_in = nc.dram_tensor("onec", [128, 1], F32, kind="ExternalInput").ap()
    ew1t = nc.dram_tensor("ew1t", [NE, NF, 128, MOE_I], F32, kind="ExternalInput").ap()
    ew3t = nc.dram_tensor("ew3t", [NE, NF, 128, MOE_I], F32, kind="ExternalInput").ap()
    ew2t = nc.dram_tensor("ew2t", [NE, NF, 128, D], F32, kind="ExternalInput").ap()
    fc1t = nc.dram_tensor("fc1t", [NF, 128, MOE_I], F32, kind="ExternalInput").ap()
    fc2t = nc.dram_tensor("fc2t", [NF, 128, MOE_I], F32, kind="ExternalInput").ap()
    fc3t = nc.dram_tensor("fc3t", [NF, 128, D], F32, kind="ExternalInput").ap()
    gwt = nc.dram_tensor("gwt", [NF, 128, NE], F32, kind="ExternalInput").ap()

    out = nc.dram_tensor("out", [NF, 128, n_tok], F32, kind="ExternalOutput").ap()
    dkind = "ExternalOutput" if dbg else "Internal"
    d_ln = nc.dram_tensor("d_ln", [NF, 128, n_tok], F32, kind=dkind).ap()
    d_a = nc.dram_tensor("d_a", [NF, 128, n_tok], F32, kind=dkind).ap()
    d_b = nc.dram_tensor("d_b", [NF, 128, n_tok], F32, kind=dkind).ap()
    d_xm = nc.dram_tensor("d_xm", [NF, 128, n_tok], BF16, kind=dkind).ap()
    d_y = nc.dram_tensor("d_y", [NF, 128, n_tok], F32, kind=dkind).ap()
    d_h = nc.dram_tensor("d_h", [NFF, 128, n_tok], BF16, kind="Internal").ap()
    d_ge = nc.dram_tensor("d_ge", [NE, n_tok], F32, kind=dkind).ap()

    def hbm(t):
        return t.rearrange("f p t -> p f t")

    es = ExitStack()
    with es:
        P = Prog(nc, es)

        A = Arena(nc, nc.sbuf_base, nc.sbuf_top)
        sb_vecs = A.alloc("vecs", [128, 416], F32)
        sb_rows = A.alloc("rows", [1, 128], F32)
        sb_bvrow = A.alloc("bvrow", [1, D], F32)
        sb_mask = A.alloc("mask", [128, 128], F32)
        sb_eye32 = A.alloc("eye32", [128, 128], F32)
        sb_eyebf = A.alloc("eyebf", [128, 128], BF16)
        sb_onesbf = A.alloc("onesbf", [128, 1], BF16)
        sb_negqk = A.alloc("negqk", [128, NL * 16], F32)
        sb_bvb = A.alloc("bvb", [128, D], F32)
        sb_zrow = A.alloc("zrow", [1, 288], BF16)
        sb_mrow = A.alloc("mrow", [1, TSA], F32)
        sb_rrow = A.alloc("rrow", [1, TSA], F32)
        sb_q2row = A.alloc("q2row", [1, TSA], F32)
        sb_trow = A.alloc("trow", [1, TSA], F32)
        WREG = 8192
        wpool = A.alloc("wpool", [128, 8 * WREG], BF16)
        wpool_off = A.off - 8 * WREG * 2
        scratchA = wpool_off + 4 * WREG * 2
        scratchB = A.off
        scratchC = wpool_off + 1 * WREG * 2
        scratchE = wpool_off + 3 * WREG * 2

        BANK = {
            "pA": nc.alloc_psum_tensor("pA", [128, 512], F32),
            "pB": nc.alloc_psum_tensor("pB", [128, 512], F32),
            "pC": nc.alloc_psum_tensor("pC", [128, 512], F32),
            "pD": nc.alloc_psum_tensor("pD", [128, 512], F32),
            "pE": nc.alloc_psum_tensor("pE", [128, 512], F32),
            "pF": nc.alloc_psum_tensor("pF", [128, 512], F32),
            "pG": nc.alloc_psum_tensor("pG", [128, 1024], BF16),
            "pH": nc.alloc_psum_tensor("pH", [128, 1024], BF16),
        }

        def wsl(reg, c0, c1):
            return wpool[:, reg * WREG + c0: reg * WREG + c1]

        def vl(l):
            return 96 * l
        VC = dict(bq=0, bk=8, bo=16, b1=24, b2=56, g1=64, be1=72, g2=80, be2=88)
        GF_C, BEF_C, GB_C = 384, 392, 400

        def vcol(c):
            return sb_vecs[:, c:c + 1]

        def vcols(l, nm_):
            base = vl(l) + VC[nm_]
            return [vcol(base + f) for f in range(NF)]

        def emit_ln(T, nf, src_ap, src_key, cp_ap, dst_ap, dst_key,
                    gain, stat_bank, bc_bank, tmp=None, bf16_dst=None,
                    bf16_key=None, rows=None, defer=None):
            sb = BANK[stat_bank]
            sb2 = BANK[bc_bank]
            if rows is None:
                r_m, r_q, r_t, r_r, rtag = sb_mrow, sb_q2row, sb_trow, sb_rrow, ""
            else:
                r_m, r_q, r_t, r_r, rtag = rows
            if T <= 256:
                s2ap = sb[0:1, 256:256 + T]
                s2key = stat_bank
                mb = sb2[:, 0:T]
                rb = sb2[:, 256:256 + T]
                rbkey = bc_bank
            else:
                s2ap = sb2[0:1, 0:T]
                s2key = bc_bank
                mb = sb2[:, 0:T]
                rb = sb[:, 0:T]
                rbkey = stat_bank

            def st1():
                P.op("scalar", lambda e: e.activation(cp_ap, src_ap, AF.Copy),
                     reads=[src_key], writes=["lncp"], inc="sS")
                mms = [lambda e, f=f: e.matmul(
                    sb[0:1, 0:T], sb_onesbf[:, 0:1], cp_ap[:, f * T:(f + 1) * T],
                    start=(f == 0), stop=(f == nf - 1)) for f in range(nf)]
                P.group("tensor", mms, reads=["lncp", "onesbf"], writes=[stat_bank])
                P.op("scalar", lambda e: e.activation(cp_ap, src_ap, AF.Square),
                     reads=[src_key], writes=["lncp"], inc="sS")
                mms = [lambda e, f=f: e.matmul(
                    s2ap, sb_onesbf[:, 0:1], cp_ap[:, f * T:(f + 1) * T],
                    start=(f == 0), stop=(f == nf - 1)) for f in range(nf)]
                st1.tok = P.group("tensor", mms, reads=["lncp", "onesbf"],
                                  writes=[s2key])

            def st2():
                P.op("vector", lambda e: e.tensor_scalar_mul(
                    r_m[:, 0:T], sb[0:1, 0:T], 1.0 / (nf * 128)),
                    reads=[stat_bank], writes=["rbm" + rtag], deps=[st1.tok], inc="sV")
                P.op("vector", lambda e: e.tensor_scalar(
                    r_q[:, 0:T], s2ap, 1.0 / (nf * 128), LN_EPS,
                    ALU.mult, ALU.add),
                    reads=[s2key], writes=["rbq" + rtag], inc="sV")
                P.op("vector", lambda e: e.tensor_tensor(
                    r_t[:, 0:T], r_m[:, 0:T], r_m[:, 0:T], ALU.mult),
                    reads=["rbm" + rtag], writes=["rbt" + rtag], inc="sV")
                P.op("vector", lambda e: e.scalar_tensor_tensor(
                    r_t[:, 0:T], r_t[:, 0:T], -1.0, r_q[:, 0:T],
                    ALU.mult, ALU.add),
                    reads=["rbq" + rtag, "rbt" + rtag], writes=["rbt" + rtag], inc="sV")
                P.op("scalar", lambda e: e.activation(
                    r_t[:, 0:T], r_t[:, 0:T], AF.Ln),
                    reads=["rbt" + rtag], writes=["rbt" + rtag], inc="sS")
                P.op("scalar", lambda e: e.activation(
                    r_r[:, 0:T], r_t[:, 0:T], AF.Exp, scale=-0.5),
                    reads=["rbt" + rtag], writes=["rbr" + rtag], inc="sS")
                P.op("vector", lambda e: e.tensor_tensor(
                    r_m[:, 0:T], r_m[:, 0:T], r_r[:, 0:T], ALU.mult),
                    reads=["rbm" + rtag, "rbr" + rtag], writes=["rbm" + rtag], inc="sV")

            def st3():
                P.op("tensor", lambda e, mb=mb: e.matmul(
                    mb, sb_rows[0:1, 0:128], r_m[:, 0:T], start=True, stop=True),
                    reads=["rbm" + rtag, "rows"], writes=[bc_bank], inc="sT")
                P.op("tensor", lambda e, rb=rb: e.matmul(
                    rb, sb_rows[0:1, 0:128], r_r[:, 0:T], start=True, stop=True),
                    reads=["rbr" + rtag, "rows"], writes=[rbkey], inc="sT")

            def st4():
                if tmp is None:
                    t_ap, t_key = dst_ap, dst_key
                else:
                    t_ap, t_key = tmp
                mb3 = mb.unsqueeze(1).broadcast_to((128, nf, T))
                rb3 = rb.unsqueeze(1).broadcast_to((128, nf, T))
                s3 = src_ap.rearrange("p (f t) -> p f t", f=nf)
                t3 = t_ap.rearrange("p (f t) -> p f t", f=nf)
                P.op("vector", lambda e: e.tensor_tensor(t3, s3, rb3, ALU.mult),
                     reads=[src_key, rbkey], writes=[t_key], inc="sV")
                d3 = dst_ap.rearrange("p (f t) -> p f t", f=nf)
                P.op("vector", lambda e: e.tensor_tensor(d3, t3, mb3, ALU.subtract),
                     reads=[t_key, bc_bank], writes=[dst_key], inc="sV")
                if gain is not None:
                    gl, bl = gain
                    for f in range(nf):
                        P.op("vector", lambda e, f=f: e.tensor_scalar(
                            dst_ap[:, f * T:(f + 1) * T], dst_ap[:, f * T:(f + 1) * T],
                            gl[f], bl[f], ALU.mult, ALU.add),
                            reads=[dst_key, "vecs"], writes=[dst_key], inc="sV")
                if bf16_dst is not None:
                    P.op("scalar", lambda e: e.activation(bf16_dst, dst_ap, AF.Copy),
                         reads=[dst_key], writes=[bf16_key], inc="sS")

            if defer is None:
                st1(); st2(); st3(); st4()
            else:
                defer.extend([st1, st2, st3, st4])
            return None

        P.op("sync", lambda e: e.dma_start(out=sb_vecs[:], in_=vecs_in),
             writes=["vecs"], inc="sP", n=16)
        P.op("sync", lambda e: e.dma_start(out=sb_rows[:], in_=rows_in[0:1, 0:128]),
             writes=["rows"], inc="sP", n=16)
        P.op("sync", lambda e: e.dma_start(out=sb_mask[:], in_=mask_in),
             writes=["mask"], inc="sP", n=16)
        P.op("sync", lambda e: e.dma_start(out=sb_eye32[:], in_=eye_in),
             writes=["eye32"], inc="sP", n=16)
        P.op("gpsimd", lambda e: e.dma_start(out=sb_eyebf[:], in_=eye_in),
             writes=["eyebf"], inc="sPW", n=16)
        P.op("gpsimd", lambda e: e.dma_start(out=sb_onesbf[:], in_=onec_in),
             writes=["onesbf"], inc="sPW", n=16)
        P.level(["vecs", "rows", "mask", "eye32"], "sP")
        P.level(["eyebf", "onesbf"], "sPW")
        P.op("vector", lambda e: e.memset(sb_zrow[:], 0.0),
             writes=["zrow"], inc="sV")
        for l in range(NL):
            P.op("vector", lambda e, l=l: e.tensor_scalar_mul(
                sb_negqk[:, l * 16:(l + 1) * 16],
                sb_vecs[:, vl(l):vl(l) + 16], -1.0),
                reads=["vecs"], writes=["negqk"], inc="sV")

        def phase_attn(l, d_src, src_tag, pre_ln):
            P.barrier_all()
            A.reset(scratchA)
            sb_x = [A.alloc("ax0", [128, NF * TSA], F32),
                    A.alloc("ax1", [128, NF * TSA], F32)]
            sb_xbf = A.alloc("axbf", [128, NF * TSA], BF16)
            qz_off = A.off
            sb_qza = A.alloc("aqza", [128, NF * TSA], BF16)
            sb_qzb = A.alloc("aqzb", [128, NF * TSA], BF16)
            sb_kf = A.alloc("akf", [128, NF * TSA], BF16)
            sb_kta = A.alloc("akta", [128, 4 * D], BF16)
            sb_ktb = A.alloc("aktb", [128, 4 * D], BF16)
            sb_v = A.alloc("av", [128, 4 * NH * 65], BF16)
            sb_sc = A.alloc("asc", [128, NH * CH], BF16)
            sb_att = A.alloc("aatt", [128, NP * CH], BF16)
            sb_attF = A.alloc("aattF", [128, NF * TSA], BF16)
            sb_S = A.alloc("aS", [128, 2 * 260], BF16)
            sb_xa = A.alloc_at("axa", [128, NF * TSA], F32, qz_off)
            sb_cp = A.alloc("acp", [128, NF * TSA], BF16)
            sb_u = A.alloc("au", [128, TSA], F32)
            sb_e = A.alloc("ae", [128, TSA], F32)
            sb_r = A.alloc("ar", [128, TSA], F32)
            sb_zc = A.alloc("azc", [128, 3 * TSA], F32)
            sb_den = A.alloc("aden", [128, 32], F32)
            v4 = sb_v[:].rearrange("p (c h e) -> p c h e", c=4, h=NH)

            for reg, w_ in ((0, wq[l]), (1, wk[l]), (2, wv[l]), (3, wo[l])):
                P.op("gpsimd", lambda e, reg=reg, w_=w_: e.dma_start(
                    out=wsl(reg, 0, NF * D).rearrange("p (f d) -> p f d", f=NF),
                    in_=hbm(w_)), writes=[("w", reg)], inc="sW", n=16)
            P.level([("w", r) for r in range(4)], "sW")

            P.op("sync", lambda e, l=l: e.dma_start(
                out=sb_bvrow[:], in_=rows_in[0:1, 128 + l * D:128 + (l + 1) * D]),
                writes=["bvrow"], inc="srb", n=16)
            bvrow = sb_bvrow[0:1, :]
            for h in range(2):
                P.op("tensor", lambda e, h=h: e.matmul(
                    BANK["pC"][:, 0:512], sb_rows[0:1, 0:128],
                    bvrow[:, h * 512:(h + 1) * 512], start=True, stop=True),
                    reads=["rows", "bvrow"], writes=["pC"], inc="sT")
                P.op("vector", lambda e, h=h: e.tensor_copy(
                    sb_bvb[:, h * 512:(h + 1) * 512], BANK["pC"][:, 0:512]),
                    reads=["pC"], writes=["bvb"], inc="sV")
            P.op("vector", lambda e: e.memset(v4[:, :, :, 64:65], 1.0),
                 writes=["vones"], inc="sV")
            kta4 = sb_kta[:].rearrange("p (c q h d) -> p c q h d", c=4, q=NP, h=2)
            ktb4 = sb_ktb[:].rearrange("p (c q h d) -> p c q h d", c=4, q=NP, h=2)
            P.op("vector", lambda e: e.memset(kta4[:, :, :, 1, :], 0.0),
                 writes=[("aktA", og) for og in range(4)], inc="sV")
            P.op("vector", lambda e: e.memset(ktb4[:, :, :, 0, :], 0.0),
                 writes=[("aktB", og) for og in range(4)], inc="sV")

            g1c, be1c = vcols(l, "g1"), vcols(l, "be1")
            boc = vcols(l, "bo")
            bqc, bkc = vcols(l, "bq"), vcols(l, "bk")
            nbq = [sb_negqk[:, l * 16 + f:l * 16 + f + 1] for f in range(NF)]
            nbk = [sb_negqk[:, l * 16 + 8 + f:l * 16 + 8 + f + 1] for f in range(NF)]

            pending = []
            for u in range(n_sup):
                us = u % sup_seq
                t0 = u * TSA
                xk = ("ax", u % 2)
                P.op("sync", lambda e, t0=t0, u=u: e.dma_start(
                    out=sb_x[u % 2][:].rearrange("p (f t) -> p f t", f=NF),
                    in_=hbm(d_src)[:, :, t0:t0 + TSA]),
                    reads=[(src_tag, u)], writes=[xk], inc=f"sxa{u % 2}", n=16)
                if pre_ln and "preln" in ATTN_SECTIONS:
                    emit_ln(TSA, NF, sb_x[u % 2][:], xk, sb_cp[:],
                            sb_x[u % 2][:], xk, None, "pC", "pD")
                src_f32 = sb_x[u % 2]
                P.op("scalar", lambda e, s=src_f32: e.activation(
                    sb_xbf[:], s[:], AF.Copy),
                    reads=[xk], writes=["axbf"], inc="sS")
                P.op("vector", lambda e: e.memset(sb_qza[64:128, :], 0.0),
                     writes=[("aqA", ot) for ot in range(NF)],
                     deps=P.buf("axa").wr(), inc="sV")
                P.op("vector", lambda e: e.memset(sb_qzb[0:64, :], 0.0),
                     writes=[("aqB", ot) for ot in range(NF)], inc="sV")

                if "qk" in ATTN_SECTIONS:
                    qk_list = ((0, None, None, nbq, bqc), (1, sb_kf, "akf", nbk, bkc))
                else:
                    qk_list = ()
                for wreg, dst, dstk, nb_, bc_ in qk_list:
                    for ot in range(NF):
                        bkk = ("pA", "pB", "pC", "pD")[ot % 4]
                        par = ot % 2
                        mms = [lambda e, kt=kt, ot=ot, bkk=bkk, wreg=wreg: e.matmul(
                            BANK[bkk][:, 0:TSA],
                            wsl(wreg, kt * D + ot * 128, kt * D + (ot + 1) * 128),
                            sb_xbf[:, kt * TSA:(kt + 1) * TSA],
                            start=(kt == 0), stop=(kt == NF - 1))
                            for kt in range(NF)]
                        P.group("tensor", mms, reads=["axbf", ("w", wreg)],
                                writes=[bkk])
                        uap = sb_u[:, 0:TSA]
                        eap = sb_e[:, 0:TSA]
                        rap = sb_r[:, 0:TSA]
                        zslot = (wreg * NF + ot) % 3
                        zc = sb_zc[:, zslot * TSA:(zslot + 1) * TSA]
                        if ot % 2 == 0:
                            P.op("scalar", lambda e, bkk=bkk, zc=zc: e.activation(
                                zc, BANK[bkk][:, 0:TSA], AF.Copy),
                                reads=[bkk], writes=[("azc", zslot)], inc="sS")
                        else:
                            P.op("vector", lambda e, bkk=bkk, zc=zc: e.tensor_copy(
                                zc, BANK[bkk][:, 0:TSA]),
                                reads=[bkk], writes=[("azc", zslot)], inc="sV")
                        if ot % 2 == 0:
                            P.op("scalar", lambda e, zc=zc, ot=ot, nb_=nb_, uap=uap: e.activation(
                                uap, zc, AF.Relu, bias=nb_[ot], scale=-1.0),
                                reads=[("azc", zslot)], writes=["au"], inc="sS")
                            P.op("scalar", lambda e, uap=uap, eap=eap: e.activation(
                                eap, uap, AF.Exp, scale=-1.0),
                                reads=["au"], writes=["ae"], inc="sS")
                            P.op("scalar", lambda e, zc=zc, ot=ot, bc_=bc_, rap=rap: e.activation(
                                rap, zc, AF.Relu, bias=bc_[ot], scale=1.0),
                                reads=[("azc", zslot)], writes=["ar"], inc="sS")
                        else:
                            P.op("vector", lambda e, zc=zc, ot=ot, bc_=bc_, uap=uap: e.tensor_scalar(
                                uap, zc, bc_[ot], 0.0, ALU.add, ALU.min),
                                reads=[("azc", zslot), "vecs"], writes=["au"], inc="sV")
                            P.op("scalar", lambda e, uap=uap, eap=eap: e.activation(
                                eap, uap, AF.Exp),
                                reads=["au"], writes=["ae"], inc="sS")
                            P.op("vector", lambda e, zc=zc, ot=ot, bc_=bc_, uap=uap, rap=rap: e.scalar_tensor_tensor(
                                rap, zc, bc_[ot], uap, ALU.add, ALU.subtract),
                                reads=[("azc", zslot), "au", "vecs"], writes=["ar"], inc="sV")
                        if wreg == 0:
                            P.op("gpsimd", lambda e, ot=ot, eap=eap, rap=rap: e.tensor_add(
                                sb_qza[0:64, ot * TSA:(ot + 1) * TSA],
                                eap[0:64, :], rap[0:64, :]),
                                reads=["ae", "ar"],
                                writes=[("aqA", ot)], inc="sG")
                            P.op("gpsimd", lambda e, ot=ot, eap=eap, rap=rap: e.tensor_add(
                                sb_qzb[64:128, ot * TSA:(ot + 1) * TSA],
                                eap[64:128, :], rap[64:128, :]),
                                reads=["ae", "ar"],
                                writes=[("aqB", ot)], inc="sG")
                        else:
                            P.op("gpsimd", lambda e, dst=dst, ot=ot, eap=eap, rap=rap: e.tensor_add(
                                dst[:, ot * TSA:(ot + 1) * TSA], eap, rap),
                                reads=["ae", "ar"], writes=[(dstk, ot)],
                                inc="sG")

                for fn_ in pending:
                    fn_()
                pending.clear()

                for c in (range(4) if "v" in ATTN_SECTIONS else ()):
                    for eh in range(2):
                        bkk = "pA" if (c * 2 + eh) % 2 == 0 else "pB"
                        mms = [lambda e, kt=kt, c=c, eh=eh, bkk=bkk: e.matmul(
                            BANK[bkk][:, 0:512],
                            sb_xbf[:, kt * TSA + c * 128:kt * TSA + (c + 1) * 128],
                            wsl(2, kt * D + eh * 512, kt * D + (eh + 1) * 512),
                            start=(kt == 0), stop=(kt == NF - 1))
                            for kt in range(NF)]
                        P.group("tensor", mms, reads=["axbf", ("w", 2)], writes=[bkk])
                        pv = BANK[bkk][:, 0:512].rearrange("p (h e) -> p h e", h=8)
                        bvv = sb_bvb[:, eh * 512:(eh + 1) * 512].rearrange(
                            "p (h e) -> p h e", h=8)
                        P.op("vector", lambda e, pv=pv, bvv=bvv, c=c, eh=eh: e.tensor_tensor(
                            v4[:, c, eh * 8:(eh + 1) * 8, 0:64], pv, bvv, ALU.add),
                            reads=[bkk, "bvb"], writes=[("av", c, eh)], inc="sV")

                for og in (range(4) if "ktr" in ATTN_SECTIONS else ()):
                    bkk = "pG" if og % 2 == 0 else "pH"
                    fns = []
                    for j in range(2):
                        ot = og * 2 + j
                        for c in range(4):
                            idx = j * 4 + c
                            fns.append(lambda e, ot=ot, c=c, idx=idx, bkk=bkk: e.transpose(
                                BANK[bkk][:, idx * 128:(idx + 1) * 128],
                                sb_kf[:, ot * TSA + c * 128:ot * TSA + (c + 1) * 128],
                                sb_eyebf[:]))
                    P.group("tensor", fns,
                            reads=[("akf", og * 2), ("akf", og * 2 + 1), "eyebf"],
                            writes=[bkk])
                    srcv = BANK[bkk][:, 0:1024].rearrange(
                        "p (j c h d) -> p j c h d", j=2, c=4, h=2)
                    P.op("scalar", lambda e, srcv=srcv, og=og: e.activation(
                        kta4[:, :, 2 * og:2 * og + 2, 0, :].rearrange(
                            "p c q d -> p q c d"),
                        srcv[:, :, :, 0, :], AF.Copy),
                        reads=[bkk], writes=[("aktA", og)], inc="sS")
                    P.op("scalar", lambda e, srcv=srcv, og=og: e.activation(
                        ktb4[:, :, 2 * og:2 * og + 2, 1, :].rearrange(
                            "p c q d -> p q c d"),
                        srcv[:, :, :, 1, :], AF.Copy),
                        reads=[bkk], writes=[("aktB", og)], inc="sS")

                for c in (range(4) if any(s.startswith("attn") for s in ATTN_SECTIONS) else ()):
                    gc = us * 4 + c
                    if gc == 0 and "attn_up" in ATTN_SECTIONS:
                        for sb_ in ("pE", "pF"):
                            P.op("tensor", lambda e, sb_=sb_: e.matmul(
                                BANK[sb_][:, 0:260], sb_zrow[0:1, 0:128],
                                sb_zrow[0:1, 0:260], start=True, stop=True),
                                reads=["zrow"], writes=[sb_], inc="sT")
                    ABANK = ("pA", "pB", "pC", "pD")
                    for g in (range(4) if "attn_sc" in ATTN_SECTIONS else ()):
                        bkk = ABANK[g]
                        mms = []
                        for j in range(4):
                            hh = 4 * g + j
                            pr, h2 = divmod(hh, 2)
                            qzsl = (sb_qza if h2 == 0 else sb_qzb)[
                                :, pr * TSA + c * 128:pr * TSA + (c + 1) * 128]
                            ksl = sb_kf[:, pr * TSA + c * 128:pr * TSA + (c + 1) * 128]
                            mms.append(lambda e, j=j, qzsl=qzsl, ksl=ksl, bkk=bkk: e.matmul(
                                BANK[bkk][:, j * 128:(j + 1) * 128],
                                ksl, qzsl, start=True, stop=True))
                        P.group("tensor", mms,
                                reads=[("aqA", 2 * g), ("aqA", 2 * g + 1),
                                       ("aqB", 2 * g), ("aqB", 2 * g + 1),
                                       ("akf", 2 * g), ("akf", 2 * g + 1)],
                                writes=[bkk])
                        if "nomask" in ATTN_SECTIONS:
                            continue
                        msk4 = sb_mask[:, 0:128].unsqueeze(1).broadcast_to((128, 4, 128))
                        P.op("vector", lambda e, g=g, bkk=bkk, msk4=msk4: e.tensor_tensor(
                            sb_sc[:, g * 512:(g + 1) * 512].rearrange(
                                "p (j t) -> p j t", j=4),
                            BANK[bkk][:, 0:512].rearrange("p (j t) -> p j t", j=4),
                            msk4, ALU.mult),
                            reads=[bkk, "mask"], writes=[("asc", g)], inc="sV")
                    for g in (range(4) if "attn_ae" in ATTN_SECTIONS else ()):
                        bkk = ABANK[g]
                        mms = []
                        rd = [("asc", g), "vones"]
                        for j2 in range(2):
                            pr = 2 * g + j2
                            rd.append(("av", c, pr // 4))
                            rd.append(("aqA", pr))
                            rd.append(("aqB", pr))
                            if gc > 0:
                                rd.append(("aSsb", pr // 4))
                            for h2 in range(2):
                                hh = 2 * pr + h2
                                base = j2 * 130 + h2 * 65
                                scl = sb_sc[:, hh * 128:(hh + 1) * 128]
                                vsl = v4[:, c, hh, 0:65]
                                mms.append(lambda e, base=base, scl=scl, vsl=vsl,
                                           bkk=bkk, last=(gc == 0): e.matmul(
                                    BANK[bkk][:, base:base + 65], scl, vsl,
                                    start=True, stop=last))
                                if gc > 0:
                                    qzsl = (sb_qza if h2 == 0 else sb_qzb)[
                                        :, pr * TSA + c * 128:pr * TSA + (c + 1) * 128]
                                    Ssl = sb_S[:, (pr // 4) * 260 + (pr % 4) * 65:
                                               (pr // 4) * 260 + (pr % 4) * 65 + 65]
                                    mms.append(lambda e, base=base, qzsl=qzsl,
                                               Ssl=Ssl, bkk=bkk: e.matmul(
                                        BANK[bkk][:, base:base + 65],
                                        qzsl, Ssl, start=False, stop=True))
                        P.group("tensor", mms, reads=rd, writes=[bkk])
                        dv = BANK[bkk][:, 0:260].rearrange("p (q e) -> p q e", e=65)
                        dslot = sb_den[:, g * 8:g * 8 + 4]
                        rslot = sb_den[:, g * 8 + 4:g * 8 + 8]
                        P.op("vector", lambda e, dv=dv, dslot=dslot: e.tensor_scalar_add(
                            dslot.unsqueeze(2), dv[:, :, 64:65], ATT_EPS),
                            reads=[bkk], writes=[("aden", g)], inc="sV")
                        P.op("vector", lambda e, dslot=dslot, rslot=rslot: e.reciprocal(
                            rslot, dslot),
                            reads=[("aden", g)], writes=[("arcp", g)], inc="sV")
                        P.op("vector", lambda e, dv=dv, g=g, rslot=rslot: e.tensor_tensor(
                            sb_att[:, g * 256:(g + 1) * 256].rearrange(
                                "p (q e) -> p q e", e=64),
                            dv[:, :, 0:64],
                            rslot.unsqueeze(2).broadcast_to((128, 4, 64)), ALU.mult),
                            reads=[bkk, ("arcp", g)], writes=[("aatt", g)], inc="sV")
                    if "attn_up" in ATTN_SECTIONS:
                        for sbank, prs in (("pE", range(0, 4)), ("pF", range(4, 8))):
                            mms = []
                            rd = [("aktA", 0), ("aktA", 1), ("aktA", 2), ("aktA", 3),
                                  ("aktB", 0), ("aktB", 1), ("aktB", 2), ("aktB", 3),
                                  ("av", c, 0), ("av", c, 1), "vones"]
                            for pr in prs:
                                o0 = (pr % 4) * 65
                                for h2 in range(2):
                                    hh = 2 * pr + h2
                                    ktz = sb_kta if h2 == 0 else sb_ktb
                                    ktsl = ktz[:, c * D + pr * 128:c * D + (pr + 1) * 128]
                                    vsl = v4[:, c, hh, 0:65]
                                    mms.append(lambda e, ktsl=ktsl, vsl=vsl, o0=o0,
                                               sbank=sbank: e.matmul(
                                        BANK[sbank][:, o0:o0 + 65],
                                        ktsl, vsl, start=False, stop=False,
                                        skip_group_check=True))
                            P.group("tensor", mms, reads=rd, writes=[sbank])
                        for bi, sbank in enumerate(("pE", "pF")):
                            P.op("vector", lambda e, bi=bi, sbank=sbank: e.tensor_copy(
                                sb_S[:, bi * 260:(bi + 1) * 260], BANK[sbank][:, 0:260]),
                                reads=[sbank], writes=[("aSsb", bi)], inc="sV")
                    if "attn_tr" not in ATTN_SECTIONS:
                        continue
                    bkk = "pG" if c % 2 == 0 else "pH"
                    fns = [lambda e, pr=pr, bkk=bkk: e.transpose(
                        BANK[bkk][:, pr * 128:(pr + 1) * 128],
                        sb_att[:, pr * 128:(pr + 1) * 128], sb_eyebf[:])
                        for pr in range(NP)]
                    P.group("tensor", fns,
                            reads=[("aatt", g) for g in range(4)] + ["eyebf"],
                            writes=[bkk])
                    srcv = BANK[bkk][:, 0:1024].rearrange("p (q t) -> p q t", q=8)
                    dstv = sb_attF[:].rearrange("p (q t) -> p q t", q=NF)[
                        :, :, c * 128:(c + 1) * 128]
                    P.op("scalar", lambda e, srcv=srcv, dstv=dstv: e.activation(
                        dstv, srcv, AF.Copy),
                         reads=[bkk], writes=[("aattF", c)], inc="sS")

                for ot in (range(NF) if "wo" in ATTN_SECTIONS else ()):
                    bkk = "pA" if ot % 2 == 0 else "pB"
                    mms = [lambda e, kt=kt, ot=ot, bkk=bkk: e.matmul(
                        BANK[bkk][:, 0:TSA],
                        wsl(3, kt * D + ot * 128, kt * D + (ot + 1) * 128),
                        sb_attF[:, kt * TSA:(kt + 1) * TSA],
                        start=(kt == 0), stop=(kt == NF - 1))
                        for kt in range(NF)]
                    P.group("tensor", mms,
                            reads=[("aattF", cc) for cc in range(4)] + [("w", 3)],
                            writes=[bkk])
                    P.op("vector", lambda e, ot=ot, bkk=bkk, s=src_f32: e.scalar_tensor_tensor(
                        sb_xa[:, ot * TSA:(ot + 1) * TSA], BANK[bkk][:, 0:TSA],
                        boc[ot], s[:, ot * TSA:(ot + 1) * TSA], ALU.add, ALU.add),
                        reads=[bkk, xk] + [("aqA", ot), ("aqB", ot)],
                        writes=["axa"], inc="sV")

                def mk_ln1(u=u, t0=t0, xk=xk):
                    if "ln1" in ATTN_SECTIONS:
                        emit_ln(TSA, NF, sb_xa[:], "axa", sb_cp[:],
                                sb_x[u % 2][:], xk, (g1c, be1c), "pC", "pD")
                    if "store" in ATTN_SECTIONS:
                        P.op("sync", lambda e, t0=t0, u=u: e.dma_start(
                            out=hbm(d_ln)[:, :, t0:t0 + TSA],
                            in_=sb_x[u % 2][:].rearrange("p (f t) -> p f t", f=NF)),
                            reads=[xk], writes=[("dln", u)], inc=f"st_a{u % 2}", n=16)
                mk_ln1()
            for fn_ in pending:
                fn_()
            pending.clear()

        def phase_b1(l):
            P.barrier_all()
            G = min(2, n_sup)
            A.reset(scratchB)
            sb_hst = [A.alloc(f"b1h{j}", [128, G * TSA], BF16) for j in range(4)]
            A.reset(wpool_off)
            sb_xl = [A.alloc("b1x0", [128, NF * TSA], F32),
                     A.alloc("b1x1", [128, NF * TSA], F32)]
            sb_xbf = A.alloc("b1xbf", [128, G * NF * TSA], BF16)
            assert A.off <= wpool_off + 4 * WREG * 2, "B1 scratch hits W1"
            P.op("gpsimd", lambda e: e.dma_start(
                out=wpool[:, 4 * WREG:4 * WREG + NF * DFF].rearrange(
                    "p (f d) -> p f d", f=NF), in_=hbm(w1[l])),
                writes=[("w", 4), ("w", 5), ("w", 6), ("w", 7)], inc="sW", n=16)
            P.level([("w", r) for r in range(4, 8)], "sW")
            b1c = [vcol(vl(l) + VC["b1"] + f) for f in range(NFF)]
            for g0 in range(0, n_sup, G):
                g = g0 // G
                Gg = min(G, n_sup - g0)
                for i in range(Gg):
                    u = g0 + i
                    t0 = u * TSA
                    P.op("sync", lambda e, t0=t0, u=u: e.dma_start(
                        out=sb_xl[u % 2][:].rearrange("p (f t) -> p f t", f=NF),
                        in_=hbm(d_ln)[:, :, t0:t0 + TSA]),
                        reads=[("dln", u)], writes=[("b1x", u % 2)],
                        inc=f"sxb{u % 2}", n=16)
                    P.op("vector", lambda e, i=i, u=u: e.tensor_copy(
                        sb_xbf[:, i * NF * TSA:(i + 1) * NF * TSA],
                        sb_xl[u % 2][:]),
                        reads=[("b1x", u % 2)], writes=[("b1bf", i)], inc="sV")
                for ht in range(NFF):
                    hp = ht % 4
                    HB = ("pA", "pB") if ht % 2 == 0 else ("pC", "pD")
                    mms = []
                    for kt in range(NF):
                        for i in range(Gg):
                            mms.append(lambda e, kt=kt, ht=ht, i=i, HB=HB: e.matmul(
                                BANK[HB[i]][:, 0:TSA],
                                wpool[:, 4 * WREG + kt * DFF + ht * 128:
                                      4 * WREG + kt * DFF + (ht + 1) * 128],
                                sb_xbf[:, i * NF * TSA + kt * TSA:
                                       i * NF * TSA + (kt + 1) * TSA],
                                start=(kt == 0), stop=(kt == NF - 1)))
                    P.group("tensor", mms,
                            reads=[("b1bf", i) for i in range(Gg)] +
                                  [("w", 4), ("w", 5), ("w", 6), ("w", 7)],
                            writes=list(HB[:Gg]))
                    for i in range(Gg):
                        P.op("scalar", lambda e, i=i, ht=ht, hp=hp, HB=HB: e.activation(
                            sb_hst[hp][:, i * TSA:(i + 1) * TSA],
                            BANK[HB[i]][:, 0:TSA],
                            AF.Gelu, bias=b1c[ht], scale=1.0),
                            reads=[HB[i]],
                            writes=[("b1h", hp)], inc="sS")
                    P.op("sync", lambda e, ht=ht, g0=g0, hp=hp, Gg=Gg: e.dma_start(
                        out=d_h[ht][:, g0 * TSA:(g0 + Gg) * TSA],
                        in_=sb_hst[hp][:, 0:Gg * TSA]),
                        reads=[("b1h", hp)], writes=[("dh", g, ht)],
                        inc=f"st_h{hp}", n=16)

        def phase_b2(l, d_dst, dst_tag, final_gf):
            P.barrier_all()
            A.reset(scratchB)
            sb_h = [nc.alloc_sbuf_tensor_at("b2h0", [128, NFF * TSA],
                                            BF16, offset=wpool_off + 4 * WREG * 2),
                    nc.alloc_sbuf_tensor_at("b2h1", [128, NFF * TSA],
                                            BF16, offset=wpool_off + 6 * WREG * 2)]
            sb_xl = [A.alloc("b2x0", [128, NF * TSA], F32),
                     A.alloc("b2x1", [128, NF * TSA], F32)]
            sb_cp = A.alloc("b2cp", [128, NF * TSA], BF16)
            rows2 = tuple(A.alloc(f"b2r{j}", [1, TSA], F32) for j in range(4)) + ("2",)
            P.op("gpsimd", lambda e: e.dma_start(
                out=wpool[:, 0:NFF * D].rearrange("p (f d) -> p f d", f=NFF),
                in_=hbm(w2[l])),
                writes=[("w", 0), ("w", 1), ("w", 2), ("w", 3)], inc="sW", n=16)
            P.level([("w", r) for r in range(4)], "sW")
            b2c = vcols(l, "b2")
            g2c, be2c = vcols(l, "g2"), vcols(l, "be2")
            gfc = [vcol(GF_C + f) for f in range(NF)]
            befc = [vcol(BEF_C + f) for f in range(NF)]
            pending = []
            for u in range(n_sup):
                t0 = u * TSA
                xk = ("b2x", u % 2)
                hk = ("b2h", u % 2)
                P.op("sync", lambda e, t0=t0, u=u: e.dma_start(
                    out=sb_xl[u % 2][:].rearrange("p (f t) -> p f t", f=NF),
                    in_=hbm(d_ln)[:, :, t0:t0 + TSA]),
                    reads=[("dln", u)], writes=[xk], inc=f"sxb{u % 2}", n=16)
                for hf in range(2):
                    P.op("sync", lambda e, t0=t0, u=u, hf=hf: e.dma_start(
                        out=sb_h[u % 2][:, hf * 16 * TSA:(hf + 1) * 16 * TSA]
                        .rearrange("p (k t) -> p k t", k=16),
                        in_=d_h.rearrange("k p t -> p k t")[
                            :, hf * 16:(hf + 1) * 16, t0:t0 + TSA]),
                        writes=[(hk, hf)], inc=f"sh{u % 2}{hf}", n=16)
                for ot in range(NF):
                    bkk = ("pA", "pB", "pE", "pF")[ot % 4]
                    for hf in range(2):
                        mms = [lambda e, kt=kt, ot=ot, bkk=bkk, u=u: e.matmul(
                            BANK[bkk][:, 0:TSA],
                            wpool[:, kt * D + ot * 128:kt * D + (ot + 1) * 128],
                            sb_h[u % 2][:, kt * TSA:(kt + 1) * TSA],
                            start=(kt == 0), stop=(kt == NFF - 1))
                            for kt in range(hf * 16, (hf + 1) * 16)]
                        P.group("tensor", mms,
                                reads=[(hk, hf), ("w", 0), ("w", 1), ("w", 2), ("w", 3)],
                                writes=[bkk])
                    P.op("vector", lambda e, ot=ot, bkk=bkk, u=u: e.scalar_tensor_tensor(
                        sb_xl[u % 2][:, ot * TSA:(ot + 1) * TSA], BANK[bkk][:, 0:TSA],
                        b2c[ot], sb_xl[u % 2][:, ot * TSA:(ot + 1) * TSA],
                        ALU.add, ALU.add),
                        reads=[bkk, xk], writes=[xk], inc="sV")
                    if pending:
                        pending.pop(0)()
                    if final_gf and pending:
                        pending.pop(0)()
                def mk_ln2(u=u, t0=t0, xk=xk):
                    st = []
                    rws = None if u % 2 == 0 else rows2
                    emit_ln(TSA, NF, sb_xl[u % 2][:], xk, sb_cp[:],
                            sb_xl[u % 2][:], xk, (g2c, be2c), "pC", "pD",
                            rows=rws, defer=st)
                    if final_gf:
                        emit_ln(TSA, NF, sb_xl[u % 2][:], xk, sb_cp[:],
                                sb_xl[u % 2][:], xk, (gfc, befc), "pC", "pD",
                                rows=rws, defer=st)

                    def store():
                        P.op("sync", lambda e, t0=t0, u=u: e.dma_start(
                            out=hbm(d_dst)[:, :, t0:t0 + TSA],
                            in_=sb_xl[u % 2][:].rearrange("p (f t) -> p f t", f=NF)),
                            reads=[xk], writes=[(dst_tag, u)], inc=f"st_b{u % 2}", n=16)
                    st.append(store)
                    return st
                pending.extend(mk_ln2())
            while pending:
                pending.pop(0)()

        def phase_c():
            P.barrier_all()
            A.reset(scratchC)
            sb_am = [A.alloc(f"cam{j}", [128, NF * TSM], F32) for j in range(2)]
            sb_xi = [A.alloc(f"cxi{j}", [128, NF * TSM], F32) for j in range(2)]
            sb_xm = A.alloc("cxm", [128, NF * TSM], BF16)
            sb_cp = A.alloc("ccp", [128, NF * TSM], BF16)
            sb_pr = A.alloc("cpr", [128, TSM], F32)
            sb_tm = A.alloc("ctm", [128, 24], F32)
            sb_t2 = A.alloc("ct2", [128, 24], F32)
            sb_p1 = A.alloc("cp1", [128, 4], F32)
            sb_p2 = A.alloc("cp2", [128, 4], F32)
            sb_sg = A.alloc("csg", [128, 4], F32)
            sb_ger = A.alloc("cger", [6, TSM], F32)
            rows2 = tuple(A.alloc(f"cr{j}", [1, TSM], F32) for j in range(4)) + ("2",)

            P.op("gpsimd", lambda e: e.dma_start(
                out=wsl(0, 0, NF * NE).rearrange("p (f d) -> p f d", f=NF),
                in_=hbm(gwt)), writes=[("w", 0)], inc="sW", n=16)
            P.level([("w", 0)], "sW")
            gbcol = vcol(GB_C)

            for u in range(nm):
                t0 = u * TSM
                amk = ("cam", u % 2)
                xik = ("cxi", u % 2)
                P.op("sync", lambda e, t0=t0, u=u: e.dma_start(
                    out=sb_am[u % 2][:].rearrange("p (f t) -> p f t", f=NF),
                    in_=hbm(d_b)[:, :, t0:t0 + TSM]),
                    reads=[("dB", u)], writes=[amk], inc=f"sca{u % 2}", n=16)
                P.op("sync", lambda e, t0=t0, u=u: e.dma_start(
                    out=sb_xi[u % 2][:].rearrange("p (f t) -> p f t", f=NF),
                    in_=hbm(x_in)[:, :, t0:t0 + TSM]),
                    writes=[xik], inc=f"sci{u % 2}", n=16)
                sbk, bck = ("pA", "pB") if u % 2 == 0 else ("pE", "pF")
                rws = None if u % 2 == 0 else rows2
                emit_ln(TSM, NF, sb_am[u % 2][:], amk, sb_cp[:],
                        sb_am[u % 2][:], amk, None, sbk, bck, rows=rws)
                P.op("vector", lambda e, u=u: e.tensor_tensor(
                    sb_xi[u % 2][:], sb_xi[u % 2][:], sb_am[u % 2][:], ALU.add),
                    reads=[amk, xik], writes=[xik], inc="sV")
                P.op("sync", lambda e, t0=t0, u=u: e.dma_start(
                    out=hbm(d_a)[:, :, t0:t0 + TSM],
                    in_=sb_xi[u % 2][:].rearrange("p (f t) -> p f t", f=NF)),
                    reads=[xik], writes=[("dA", u)], inc=f"st_c5{u % 2}", n=16)
                emit_ln(TSM, NF, sb_xi[u % 2][:], xik, sb_cp[:],
                        sb_am[u % 2][:], amk, None, sbk, bck,
                        bf16_dst=sb_xm[:], bf16_key="cxm", rows=rws)
                P.op("sync", lambda e, t0=t0: e.dma_start(
                    out=hbm(d_xm)[:, :, t0:t0 + TSM],
                    in_=sb_xm[:].rearrange("p (f t) -> p f t", f=NF)),
                    reads=["cxm"], writes=[("dxm", u)], inc="st_cxm", n=16)
                mms = [lambda e, kt=kt: e.matmul(
                    BANK["pC"][0:NE, 0:TSM], wsl(0, kt * NE, (kt + 1) * NE),
                    sb_xm[:, kt * TSM:(kt + 1) * TSM],
                    start=(kt == 0), stop=(kt == NF - 1)) for kt in range(NF)]
                P.group("tensor", mms, reads=["cxm", ("w", 0)], writes=["pC"])
                P.op("scalar", lambda e: e.activation(
                    sb_pr[0:NE, 0:TSM], BANK["pC"][0:NE, 0:TSM], AF.Sigmoid,
                    bias=gbcol[0:NE, :], scale=1.0),
                    reads=["pC"], writes=["cpr"], inc="sS")
                fns = [lambda e, cc=cc: e.transpose(
                    BANK["pD"][:, cc * NE:(cc + 1) * NE],
                    sb_pr[0:NE, cc * 128:(cc + 1) * 128], sb_eye32[0:NE, 0:NE])
                    for cc in range(4)]
                P.group("tensor", fns, reads=["cpr", "eye32"], writes=["pD"])
                P.op("vector", lambda e: e.tensor_copy(sb_tm[:], BANK["pD"][:, 0:24]),
                     reads=["pD"], writes=["ctm"], inc="sV")
                tm3 = sb_tm[:].rearrange("p (c e) -> p c e", c=4)
                t23 = sb_t2[:].rearrange("p (c e) -> p c e", c=4)
                P.op("vector", lambda e, tm3=tm3: e.tensor_reduce(
                    sb_p1[:].unsqueeze(2), tm3, AX.X, ALU.max),
                    reads=["ctm"], writes=["cp1"], inc="sV")
                P.op("vector", lambda e, tm3=tm3, t23=t23: e.tensor_tensor(
                    t23, tm3, sb_p1[:].unsqueeze(2).broadcast_to((128, 4, NE)),
                    ALU.is_ge), reads=["ctm", "cp1"], writes=["ct2"], inc="sV")
                P.op("vector", lambda e: e.scalar_tensor_tensor(
                    sb_t2[:], sb_t2[:], -10.0, sb_tm[:], ALU.mult, ALU.add),
                    reads=["ct2", "ctm"], writes=["ct2"], inc="sV")
                P.op("vector", lambda e, t23=t23: e.tensor_reduce(
                    sb_p2[:].unsqueeze(2), t23, AX.X, ALU.max),
                    reads=["ct2"], writes=["cp2"], inc="sV")
                P.op("vector", lambda e: e.tensor_tensor(
                    sb_sg[:], sb_p1[:], sb_p2[:], ALU.add),
                    reads=["cp1", "cp2"], writes=["csg"], inc="sV")
                P.op("vector", lambda e: e.tensor_scalar_add(
                    sb_sg[:], sb_sg[:], GATE_EPS),
                    reads=["csg"], writes=["csg"], inc="sV")
                P.op("vector", lambda e: e.reciprocal(sb_sg[:], sb_sg[:]),
                     reads=["csg"], writes=["csg"], inc="sV")
                P.op("vector", lambda e, tm3=tm3, t23=t23: e.tensor_tensor(
                    t23, tm3, sb_p2[:].unsqueeze(2).broadcast_to((128, 4, NE)),
                    ALU.is_ge), reads=["ctm", "cp2"], writes=["ct2"], inc="sV")
                P.op("vector", lambda e: e.tensor_tensor(
                    sb_t2[:], sb_t2[:], sb_tm[:], ALU.mult),
                    reads=["ct2", "ctm"], writes=["ct2"], inc="sV")
                P.op("vector", lambda e, t23=t23: e.tensor_tensor(
                    t23, t23, sb_sg[:].unsqueeze(2).broadcast_to((128, 4, NE)),
                    ALU.mult), reads=["ct2", "csg"], writes=["ct2"], inc="sV")
                fns = [lambda e, cc=cc: e.transpose(
                    BANK["pD"][0:NE, cc * 128:(cc + 1) * 128],
                    sb_t2[:, cc * NE:(cc + 1) * NE], sb_eye32[:, 0:128])
                    for cc in range(4)]
                P.group("tensor", fns, reads=["ct2", "eye32"], writes=["pD"])
                P.op("vector", lambda e: e.tensor_copy(
                    sb_ger[:], BANK["pD"][0:NE, 0:TSM]),
                    reads=["pD"], writes=["cger"], inc="sV")
                P.op("sync", lambda e, t0=t0: e.dma_start(
                    out=d_ge[:, t0:t0 + TSM], in_=sb_ger[:]),
                    reads=["cger"], writes=[("dge", u)], inc="st_cge", n=16)

        def phase_expert(e_):
            P.barrier_all()
            A.reset(scratchE)
            G = min(2, nm)
            sb_xmb = [A.alloc("ex0", [128, NF * TSM], BF16),
                      A.alloc("ex1", [128, NF * TSM], BF16)]
            sb_s1 = A.alloc("es1", [128, 2 * TSM], F32)
            sb_he = A.alloc("ehe", [128, G * 8 * TSM], BF16)
            sb_geb = A.alloc("egeb", [128, G * TSM], F32)
            sb_ger = A.alloc("eger", [1, n_tok], F32)
            sb_y = [A.alloc("ey0", [128, NF * TSM], F32),
                    A.alloc("ey1", [128, NF * TSM], F32)]

            shared = e_ < 0
            if shared:
                srcs = (fc1t, fc2t, fc3t)
            else:
                srcs = (ew1t[e_], ew3t[e_], ew2t[e_])
            for reg, w_ in enumerate(srcs):
                P.op("gpsimd", lambda e, reg=reg, w_=w_: e.dma_start(
                    out=wsl(reg, 0, NF * 1024).rearrange("p (f d) -> p f d", f=NF),
                    in_=hbm(w_)), writes=[("w", reg)], inc="sW", n=16)
            P.level([("w", r) for r in range(3)], "sW")
            if not shared:
                P.op("sync", lambda e: e.dma_start(out=sb_ger[:], in_=d_ge[e_:e_ + 1, :]),
                     reads=[("dge", uu) for uu in range(nm)], writes=["eger"],
                     inc="sge", n=16)

            for g0 in range(0, nm, G):
                Gg = min(G, nm - g0)
                for i in range(Gg):
                    u = g0 + i
                    t0 = u * TSM
                    P.op("sync", lambda e, t0=t0, i=i: e.dma_start(
                        out=sb_xmb[i][:].rearrange("p (f t) -> p f t", f=NF),
                        in_=hbm(d_xm)[:, :, t0:t0 + TSM]),
                        reads=[("dxm", u)], writes=[("exm", i)],
                        inc=f"sxe{i}", n=16)
                    if not shared:
                        P.op("tensor", lambda e, t0=t0, i=i: e.matmul(
                            BANK[("pE", "pF")[i]][:, 0:TSM], sb_rows[0:1, 0:128],
                            sb_ger[0:1, t0:t0 + TSM], start=True, stop=True),
                            reads=["eger", "rows"], writes=[("pE", "pF")[i]], inc="sT")
                        P.op("vector", lambda e, i=i: e.tensor_copy(
                            sb_geb[:, i * TSM:(i + 1) * TSM],
                            BANK[("pE", "pF")[i]][:, 0:TSM]),
                            reads=[("pE", "pF")[i]], writes=[("egeb", i)], inc="sV")
                for ht in range(8):
                    for reg, banks in ((0, ("pA", "pB")), (1, ("pC", "pD"))):
                        mms = []
                        for kt in range(NF):
                            for i in range(Gg):
                                mms.append(lambda e, kt=kt, ht=ht, reg=reg, i=i,
                                           banks=banks: e.matmul(
                                    BANK[banks[i]][:, 0:TSM],
                                    wsl(reg, kt * 1024 + ht * 128,
                                        kt * 1024 + (ht + 1) * 128),
                                    sb_xmb[i][:, kt * TSM:(kt + 1) * TSM],
                                    start=(kt == 0), stop=(kt == NF - 1)))
                        P.group("tensor", mms,
                                reads=[("exm", 0), ("exm", 1), ("w", reg)],
                                writes=list(banks[:Gg]))
                    for i in range(Gg):
                        b1k = ("pA", "pB")[i]
                        b3k = ("pC", "pD")[i]
                        he_sl = sb_he[:, (i * 8 + ht) * TSM:(i * 8 + ht + 1) * TSM]
                        s1 = sb_s1[:, i * TSM:(i + 1) * TSM]
                        P.op("scalar", lambda e, b1k=b1k, s1=s1: e.activation(
                            s1, BANK[b1k][:, 0:TSM], AF.Silu),
                            reads=[b1k], writes=[("es1", i)], inc="sS")
                        if shared:
                            P.op("vector", lambda e, he_sl=he_sl, b3k=b3k, s1=s1: e.tensor_tensor(
                                he_sl, s1, BANK[b3k][:, 0:TSM], ALU.mult),
                                reads=[("es1", i), b3k], writes=[("ehe", i, ht)], inc="sV")
                        else:
                            P.op("vector", lambda e, b3k=b3k, s1=s1: e.tensor_tensor(
                                s1, s1, BANK[b3k][:, 0:TSM], ALU.mult),
                                reads=[("es1", i), b3k], writes=[("es1", i)], inc="sV")
                            P.op("vector", lambda e, he_sl=he_sl, i=i, s1=s1: e.tensor_tensor(
                                he_sl, s1,
                                sb_geb[:, i * TSM:(i + 1) * TSM], ALU.mult),
                                reads=[("es1", i), ("egeb", i)], writes=[("ehe", i, ht)],
                                inc="sV")
                for ot in range(NF):
                    banks = ("pE", "pF") if ot % 2 == 0 else ("pA", "pB")
                    mms = []
                    for kt in range(8):
                        for i in range(Gg):
                            mms.append(lambda e, kt=kt, ot=ot, i=i, banks=banks: e.matmul(
                                BANK[banks[i]][:, 0:TSM],
                                wsl(2, kt * D + ot * 128, kt * D + (ot + 1) * 128),
                                sb_he[:, (i * 8 + kt) * TSM:(i * 8 + kt + 1) * TSM],
                                start=(kt == 0), stop=(kt == 7)))
                    P.group("tensor", mms,
                            reads=[("ehe", i, kt) for i in range(Gg) for kt in range(8)] +
                                  [("w", 2)],
                            writes=[banks[0], banks[1]])
                    for i in range(Gg):
                        P.op("scalar", lambda e, ot=ot, i=i, banks=banks: e.activation(
                            sb_y[i][:, ot * TSM:(ot + 1) * TSM],
                            BANK[banks[i]][:, 0:TSM], AF.Copy),
                            reads=[banks[i]], writes=[("ey", i)], inc="sS")
                for i in range(Gg):
                    u = g0 + i
                    t0 = u * TSM
                    P.op("gpsimd", lambda e, t0=t0, i=i, shared=shared: e.dma_start(
                        out=hbm(d_y)[:, :, t0:t0 + TSM],
                        in_=sb_y[i][:].rearrange("p (f t) -> p f t", f=NF),
                        accum_op=(ALU.bypass if shared else ALU.add)),
                        reads=[("ey", i)] + ([] if shared else [("dy", u)]),
                        writes=[("dy", u)], inc=f"st_y{i}", n=16)

        def phase_final():
            P.barrier_all()
            A.reset(scratchC)
            sb_y = [A.alloc(f"fy{j}", [128, NF * TSM], F32) for j in range(2)]
            sb_x5 = [A.alloc(f"fx{j}", [128, NF * TSM], F32) for j in range(2)]
            sb_cp = A.alloc("fcp", [128, NF * TSM], BF16)
            rows2 = tuple(A.alloc(f"fr{j}", [1, TSM], F32) for j in range(4)) + ("2",)
            for u in range(nm):
                t0 = u * TSM
                yk = ("fy", u % 2)
                xk = ("fx", u % 2)
                P.op("sync", lambda e, t0=t0, u=u: e.dma_start(
                    out=sb_y[u % 2][:].rearrange("p (f t) -> p f t", f=NF),
                    in_=hbm(d_y)[:, :, t0:t0 + TSM]),
                    reads=[("dy", u)], writes=[yk], inc=f"sfy{u % 2}", n=16)
                P.op("sync", lambda e, t0=t0, u=u: e.dma_start(
                    out=sb_x5[u % 2][:].rearrange("p (f t) -> p f t", f=NF),
                    in_=hbm(d_a)[:, :, t0:t0 + TSM]),
                    reads=[("dA", u)], writes=[xk], inc=f"sfx{u % 2}", n=16)
                sbk, bck = ("pA", "pB") if u % 2 == 0 else ("pC", "pD")
                rws = None if u % 2 == 0 else rows2
                emit_ln(TSM, NF, sb_y[u % 2][:], yk, sb_cp[:],
                        sb_y[u % 2][:], yk, None, sbk, bck, rows=rws)
                P.op("vector", lambda e, u=u: e.tensor_tensor(
                    sb_x5[u % 2][:], sb_x5[u % 2][:], sb_y[u % 2][:], ALU.add),
                    reads=[yk, xk], writes=[xk], inc="sV")
                P.op("sync", lambda e, t0=t0, u=u: e.dma_start(
                    out=hbm(out)[:, :, t0:t0 + TSM],
                    in_=sb_x5[u % 2][:].rearrange("p (f t) -> p f t", f=NF)),
                    reads=[xk], inc=f"st_f{u % 2}", n=16)

        phases = [
            lambda: phase_attn(0, x_in, "dx", True),
            lambda: phase_b1(0), lambda: phase_b2(0, d_a, "dA", False),
            lambda: phase_attn(1, d_a, "dA", False),
            lambda: phase_b1(1), lambda: phase_b2(1, d_b, "dB", False),
            lambda: phase_attn(2, d_b, "dB", False),
            lambda: phase_b1(2), lambda: phase_b2(2, d_a, "dA", False),
            lambda: phase_attn(3, d_a, "dA", False),
            lambda: phase_b1(3), lambda: phase_b2(3, d_b, "dB", True),
            phase_c,
            lambda: phase_expert(-1),
        ] + [(lambda e_=e_: phase_expert(e_)) for e_ in range(NE)] + [phase_final]
        for ph in phases[:upto]:
            ph()

        for s_ in [s for s in P.cnt if s.startswith("st_")]:
            eng_ = "gpsimd" if s_.startswith("st_y") else "sync"
            P.q[eng_].append(lambda e, s_=s_, v_=P.cnt[s_]: e.wait_ge(P.sems[s_], v_))

        block = es.enter_context(nc.Block())

        def _skip_redundant_ldweights():
            import bass_rust
            n_skip = 0
            for f_ in nc.m.functions:
                for blk_ in f_.blocks:
                    il = blk_.instructions
                    prev_key = None
                    changed = False
                    for idx, ins_ in enumerate(il):
                        if isinstance(ins_, mybir.InstMatmult):
                            if ins_.is_transpose:
                                prev_key = None
                                continue
                            try:
                                wop = ins_.ins[1]
                                key = (str(wop), str(ins_.tile_position),
                                       str(ins_.perf_mode))
                            except Exception:
                                prev_key = None
                                continue
                            if "float32" in key[0]:
                                prev_key = None
                                continue
                            if prev_key is not None and key == prev_key:
                                il[idx] = ins_.__replace__(ldweights=False)
                                changed = True
                                n_skip += 1
                            prev_key = key
                        elif not isinstance(ins_, (mybir.InstEventSemaphore,)):
                            if type(ins_).__name__ not in (
                                    "InstEventSemaphore", "InstNop"):
                                prev_key = None
                    if changed:
                        blk_.instructions = il
            return n_skip

        @block.sync
        def _(eng):
            for fn in P.q["sync"]:
                fn(eng)

        @block.gpsimd
        def _(eng):
            for fn in P.q["gpsimd"]:
                fn(eng)

        @block.tensor
        def _(eng):
            for fn in P.q["tensor"]:
                fn(eng)

        @block.vector
        def _(eng):
            for fn in P.q["vector"]:
                fn(eng)

        @block.scalar
        def _(eng):
            for fn in P.q["scalar"]:
                fn(eng)

        if SKIP_LDW:
            _skip_redundant_ldweights()

    return nc


_CACHE = {}


def _get_prog(n_tok, n_seq, dbg=False):
    key = (n_tok, n_seq, dbg)
    if key not in _CACHE:
        _CACHE[key] = build_program(n_tok, n_seq, dbg)
    return _CACHE[key]


def _pack_cols(vec):
    v = np.asarray(vec, np.float32)
    return np.ascontiguousarray(v.reshape(-1, 128).T)


def host_inputs(kw):
    f32 = np.float32
    base = {}
    for l in range(NL):
        base[f"wq{l}"] = np.ascontiguousarray(np.asarray(kw["L0_attn_Wq"][l], f32)).reshape(NF, 128, D)
        base[f"wk{l}"] = np.ascontiguousarray(np.asarray(kw["L0_attn_Wk"][l], f32)).reshape(NF, 128, D)
        base[f"wv{l}"] = np.ascontiguousarray(np.asarray(kw["L0_attn_Wv"][l], f32)).reshape(NF, 128, D)
        base[f"wo{l}"] = np.ascontiguousarray(np.asarray(kw["L0_attn_Wo"][l], f32)).reshape(NF, 128, D)
        base[f"w1_{l}"] = np.ascontiguousarray(np.asarray(kw["L0_attn_W1"][l], f32)).reshape(NF, 128, DFF)
        base[f"w2_{l}"] = np.ascontiguousarray(np.asarray(kw["L0_attn_W2"][l], f32)).reshape(NFF, 128, D)
    vecs = np.zeros((128, 416), f32)
    for l in range(NL):
        b = 96 * l
        vecs[:, b + 0:b + 8] = _pack_cols(kw["L0_attn_bq"][l])
        vecs[:, b + 8:b + 16] = _pack_cols(kw["L0_attn_bk"][l])
        vecs[:, b + 16:b + 24] = _pack_cols(kw["L0_attn_bo"][l])
        vecs[:, b + 24:b + 56] = _pack_cols(kw["L0_attn_b1"][l])
        vecs[:, b + 56:b + 64] = _pack_cols(kw["L0_attn_b2"][l])
        vecs[:, b + 64:b + 72] = _pack_cols(kw["L0_attn_g1"][l])
        vecs[:, b + 72:b + 80] = _pack_cols(kw["L0_attn_be1"][l])
        vecs[:, b + 80:b + 88] = _pack_cols(kw["L0_attn_g2"][l])
        vecs[:, b + 88:b + 96] = _pack_cols(kw["L0_attn_be2"][l])
    vecs[:, 384:392] = _pack_cols(kw["L0_attn_gf"])
    vecs[:, 392:400] = _pack_cols(kw["L0_attn_bef"])
    gb = np.zeros(128, f32)
    gb[:NE] = np.asarray(kw["L0_ffn_gate_b"], f32)
    vecs[:, 400] = gb
    base["vecs"] = vecs
    rows = np.zeros((1, 128 + NL * D), f32)
    rows[0, :128] = 1.0
    for l in range(NL):
        rows[0, 128 + l * D:128 + (l + 1) * D] = np.asarray(kw["L0_attn_bv"][l], f32)
    base["rows"] = rows
    base["maskT"] = np.triu(np.ones((128, 128), f32))
    base["eye"] = np.eye(128, dtype=f32)
    base["onec"] = np.ones((128, 1), f32)
    base["ew1t"] = np.ascontiguousarray(np.asarray(kw["L0_ffn_ew1"], f32).transpose(0, 2, 1)).reshape(NE, NF, 128, MOE_I)
    base["ew3t"] = np.ascontiguousarray(np.asarray(kw["L0_ffn_ew3"], f32).transpose(0, 2, 1)).reshape(NE, NF, 128, MOE_I)
    base["ew2t"] = np.ascontiguousarray(np.asarray(kw["L0_ffn_ew2"], f32).transpose(0, 2, 1)).reshape(NE, NF, 128, D)
    base["fc1t"] = np.ascontiguousarray(np.asarray(kw["L0_ffn_fc1"], f32).T).reshape(NF, 128, MOE_I)
    base["fc2t"] = np.ascontiguousarray(np.asarray(kw["L0_ffn_fc2"], f32).T).reshape(NF, 128, MOE_I)
    base["fc3t"] = np.ascontiguousarray(np.asarray(kw["L0_ffn_fc3"], f32).T).reshape(NF, 128, D)
    base["gwt"] = np.ascontiguousarray(np.asarray(kw["L0_ffn_gate_w"], f32).T).reshape(NF, 128, NE)
    return base


def kernel(**kw):
    n_cores = 8
    x = np.asarray(kw["L0_attn_x"], np.float32)
    B, S, _ = x.shape
    per = B // n_cores
    n_tok = per * S
    base = host_inputs(kw)
    nc = _get_prog(n_tok, per)
    in_maps = []
    for c in range(n_cores):
        xc = np.ascontiguousarray(
            x[c * per:(c + 1) * per].reshape(n_tok, D).T).reshape(NF, 128, n_tok)
        in_maps.append({**base, "x": xc})
    res = run_bass_kernel_spmd(nc, in_maps, core_ids=list(range(n_cores)))
    outs = []
    for c in range(n_cores):
        o = np.asarray(res.results[c]["out"]).reshape(D, n_tok)
        outs.append(np.ascontiguousarray(o.T).reshape(per, S, D))
    return np.concatenate(outs, axis=0).astype(np.float32)
